# version 06
# baseline (speedup 1.0000x reference)
import numpy as np
import ml_dtypes
import concourse.bass as bass
import concourse.mybir as mybir
from concourse.bass_utils import run_bass_kernel_spmd

F32 = mybir.dt.float32
BF16 = mybir.dt.bfloat16
AF = mybir.ActivationFunctionType
ALU = mybir.AluOpType
AX = mybir.AxisListType
bfloat16 = ml_dtypes.bfloat16

HD = 2048
NH = 16
S = 4096
TPC = 1024
NE = 32
NSC = 16
IE = 768
NEG = 16
INV_SQRT_DH = float(1.0 / np.sqrt(128.0))
KB = 1024


def chunk_assign(g):
    return [4 * j + ((g + j) % 4) for j in range(8)]


def score_parts(kc):
    qs = 128 * (kc // 4)
    if qs < 512:
        return [(qs, 512 - qs), (512, 512)]
    return [(qs, 1024 - qs)]


class Sem:
    def __init__(self, nc, name):
        self.h = nc.alloc_semaphore(name)
        self.n = 0

    def inc(self, k=1):
        self.n += k
        return self.n


class Builder:
    def __init__(self, debug=0, neg=NEG):
        self.debug = debug
        self.neg = neg
        nc = bass.Bass("TRN2", target_bir_lowering=False, debug=False)
        self.nc = nc
        self.ops = {k: [] for k in ("sync", "gpsimd", "vector", "scalar", "tensor")}
        self.declare_io()
        self.declare_sbuf()
        self.declare_psum()
        self.declare_sems()

    def emit(self, eng, fn):
        self.ops[eng].append(fn)

    def wait(self, eng, sem, val):
        if val > 0:
            self.emit(eng, lambda e, s=sem.h, v=val: e.wait_ge(s, v))

    def dma(self, eng, out, in_, sem, **kw):
        sem.inc(16)
        self.emit(eng, lambda e, o=out, i=in_, s=sem.h, k=kw:
                  e.dma_start(out=o, in_=i, **k).then_inc(s, 16))

    def pe(self, fn, inc=False):
        if inc:
            self.s_pe.inc()
            self.emit("tensor", lambda e, f=fn: f(e).then_inc(self.s_pe.h, 1))
        else:
            self.emit("tensor", lambda e, f=fn: f(e))
        return self.s_pe.n

    def act(self, fn):
        self.s_act.inc()
        self.emit("scalar", lambda e, f=fn: f(e).then_inc(self.s_act.h, 1))
        return self.s_act.n

    def dve(self, fn):
        self.s_dve.inc()
        self.emit("vector", lambda e, f=fn: f(e).then_inc(self.s_dve.h, 1))
        return self.s_dve.n

    def declare_io(self):
        nc = self.nc
        D = nc.dram_tensor
        self.xb = D("xb", [S, HD], F32, kind="ExternalInput").ap()
        self.qx = D("qx", [TPC, HD], F32, kind="ExternalInput").ap()
        self.masks = D("masks", [32, 128, 128], BF16, kind="ExternalInput").ap()
        self.Wq = D("Wq", [HD, HD], F32, kind="ExternalInput").ap()
        self.Wk = D("Wk", [HD, HD], F32, kind="ExternalInput").ap()
        self.Wv = D("Wv", [HD, HD], F32, kind="ExternalInput").ap()
        self.Wo = D("Wo", [HD, HD], F32, kind="ExternalInput").ap()
        self.bq = D("bq", [128, 16], F32, kind="ExternalInput").ap()
        self.bk = D("bk", [128, 16], F32, kind="ExternalInput").ap()
        self.bv = D("bv", [128, 16], F32, kind="ExternalInput").ap()
        self.gwt = D("gwt", [HD, NE], F32, kind="ExternalInput").ap()
        self.Wg = D("Wg", [NE, HD, IE], F32, kind="ExternalInput").ap()
        self.Wu = D("Wu", [NE, HD, IE], F32, kind="ExternalInput").ap()
        self.Wd = D("Wd", [NE, IE, HD], F32, kind="ExternalInput").ap()
        self.cWg = D("cWg", [NSC, HD, 128], F32, kind="ExternalInput").ap()
        self.cWu = D("cWu", [NSC, HD, 128], F32, kind="ExternalInput").ap()
        self.cWd = D("cWd", [NSC, 128, HD], F32, kind="ExternalInput").ap()
        self.ident_bf = D("ident_bf", [128, 128], BF16, kind="ExternalInput").ap()
        self.ident32 = D("ident32", [128, 128], F32, kind="ExternalInput").ap()
        self.ones_col = D("ones_col", [128, 1], BF16, kind="ExternalInput").ap()
        self.onesr32 = D("onesr32", [1, 128], F32, kind="ExternalInput").ap()
        self.onesrbf = D("onesrbf", [1, 128], BF16, kind="ExternalInput").ap()

        self.out = D("out", [TPC, HD], F32, kind="ExternalOutput").ap()

        self.xb_bf = D("xb_bf", [S, HD], BF16, kind="Internal").ap()
        self.qx_bf = D("qx_bf", [TPC, HD], BF16, kind="Internal").ap()
        self.K_dram = D("K_dram", [NH, 128, S], BF16, kind="Internal").ap()
        self.V_dram = D("V_dram", [NH, 128, S], BF16, kind="Internal").ap()
        self.x2_dram = D("x2_dram", [TPC, HD], F32, kind="Internal").ap()
        self.cw_dram = D("cw_dram", [NE, TPC], BF16, kind="Internal").ap()
        self.cws_dram = D("cws_dram", [NSC, TPC], BF16, kind="Internal").ap()
        self.acc_dram = D("acc_dram", [TPC, HD], F32, kind="Internal").ap()

        if self.debug:
            self.d_K = D("d_K", [NH, 128, S], BF16, kind="ExternalOutput").ap()
            self.d_V = D("d_V", [NH, 128, S], BF16, kind="ExternalOutput").ap()
            self.d_Q = D("d_Q", [128, NH * TPC], BF16, kind="ExternalOutput").ap()
            self.d_ctx = D("d_ctx", [128, NH * TPC], BF16, kind="ExternalOutput").ap()
            self.d_x2 = D("d_x2", [TPC, HD], F32, kind="ExternalOutput").ap()
            self.d_cw = D("d_cw", [NE, TPC], BF16, kind="ExternalOutput").ap()
            self.d_cws = D("d_cws", [NSC, TPC], BF16, kind="ExternalOutput").ap()
            self.d_acc = D("d_acc", [TPC, HD], F32, kind="ExternalOutput").ap()
            self.d_aeg = D("d_aeg", [128, 12 * 1024], BF16, kind="ExternalOutput").ap()
            self.d_asm = D("d_asm", [128, 1024], BF16, kind="ExternalOutput").ap()
            self.d_cwb = D("d_cwb", [128, 1024], F32, kind="ExternalOutput").ap()

    def declare_sbuf(self):
        nc = self.nc
        A = lambda *a: nc.alloc_sbuf_tensor(*a).ap()
        self.ident_sb = A("ident_sb", [128, 128], BF16)
        self.ident32_sb = A("ident32_sb", [128, 128], F32)
        self.ones_sb = A("ones_sb", [128, 1], BF16)
        self.onesr32_sb = A("onesr32_sb", [1, 128], F32)
        self.onesrbf_sb = A("onesrbf_sb", [1, 128], BF16)
        self.bq_sb = A("bq_sb", [128, 16], F32)
        self.bk_sb = A("bk_sb", [128, 16], F32)
        self.bv_sb = A("bv_sb", [128, 16], F32)
        self.gw_sb = A("gw_sb", [128, NH * NE], BF16)

        base = nc._sbuf_addr_for_side("left")
        base = (base + 127) & ~127

        def At(name, shape, dtype, off_kb):
            return nc.alloc_sbuf_tensor_at(name, shape, dtype,
                                           offset=base + int(off_kb * KB)).ap()

        self.XF = [At("XF0", [128, 16 * 512], BF16, 0), At("XF1", [128, 16 * 512], BF16, 16)]
        self.masks_sb = At("masks_sb", [128, 32 * 128], BF16, 0)
        self.EXP = At("EXP", [128, 3 * 1024], BF16, 8)
        self.den_sb = At("den_sb", [1, 1024], F32, 14)
        self.recip_sb = At("recip_sb", [1, 1024], F32, 18)
        self.rb_sb = At("rb_sb", [128, 512], F32, 22)
        self.recipbf_sb = At("recipbf_sb", [1, 1024], BF16, 24)
        self.x2f_sb = At("x2f_sb", [128, 16 * 1024], BF16, 0)
        self.Wv_sb = At("Wv_sb", [128, 16 * 2048], BF16, 32)
        self.Q_sb = At("Q_sb", [128, NH * 1024], BF16, 32)
        self.ctx_sb = At("ctx_sb", [128, NH * 1024], BF16, 64)
        self.lg_sb = At("lg_sb", [128, NE], F32, 32)
        self.work_sb = At("work_sb", [128, NE], F32, 32.25)
        self.eq_sb = At("eq_sb", [128, NE], F32, 32.5)
        self.msk_sb = At("msk_sb", [128, NE], F32, 32.75)
        self.ex_sb = At("ex_sb", [128, NE], F32, 33.0)
        self.wm_sb = At("wm_sb", [128, NE], F32, 33.25)
        self.mpos_sb = At("mpos_sb", [128, 8], F32, 33.5)
        self.rsc_sb = At("rsc_sb", [128, 8], F32, 33.75)
        self.cwt_sb = At("cwt_sb", [128, NE], F32, 34.0)
        self.cwst_sb = At("cwst_sb", [128, NSC], F32, 34.25)
        self.cwtr_sb = At("cwtr_sb", [32, 128], BF16, 34.5)
        self.cwstr_sb = At("cwstr_sb", [16, 128], BF16, 35.0)
        self.AEG = At("AEG", [128, 12 * 1024], BF16, 32)
        self.ASM = At("ASM", [128, 1024], BF16, 56)
        self.SG = At("SG", [128, 512], F32, 58)
        self.T1 = At("T1", [128, 512], F32, 60)
        self.cwb_sb = At("cwb_sb", [128, 1024], F32, 62)
        self.EIN = [At("EIN0", [128, 1024], F32, 66), At("EIN1", [128, 1024], F32, 70)]
        self.EOUT = [At("EOUT0", [128, 1024], F32, 74), At("EOUT1", [128, 1024], F32, 78)]
        self.W1 = At("W1", [128, 16 * 2048], BF16, 96)
        self.KH = [At("KH0", [128, S], BF16, 96), At("KH1", [128, S], BF16, 104)]
        self.VT = [At("VT0", [128, S], BF16, 128), At("VT1", [128, S], BF16, 136)]
        self.WGU = [At("WGU%d" % i, [128, 16 * 384], BF16, 96 + 12 * i) for i in range(4)]
        self.WD = [At("WD0", [128, 6 * 2048], BF16, 144), At("WD1", [128, 6 * 2048], BF16, 168)]
        self.KST = [At("KST%d" % i, [128, 512], BF16, 160 + i) for i in range(4)]
        self.TS = [At("TS%d" % i, [128, 2048], F32, 160 + 8 * i) for i in range(4)]
        self.TS.append(At("TS4", [128, 2048], F32, 196))
        self.TB = At("TB", [128, 2048], BF16, 192)
        self.SMG = At("SMG", [128, 16 * 128], BF16, 192)
        self.SMU = At("SMU", [128, 16 * 128], BF16, 196)
        self.SMD = At("SMD", [128, 2048], BF16, 200)
        self.CWR = [At("CWR0", [1, 512], BF16, 86.0), At("CWR1", [1, 512], BF16, 88.0)]
        self.s1_sb = At("s1_sb", [128, 1], F32, 204.25)
        self.s2_sb = At("s2_sb", [128, 1], F32, 204.3125)
        self.m_sb = At("m_sb", [128, 1], F32, 204.375)
        self.v_sb = At("v_sb", [128, 1], F32, 204.4375)
        self.t_sb = At("t_sb", [128, 1], F32, 204.5)
        self.rstd_sb = At("rstd_sb", [128, 1], F32, 204.5625)

    def declare_psum(self):
        nc = self.nc
        self.PB1 = nc.alloc_psum_tensor("PB1", [128, 1024], F32).ap()
        self.PB2 = nc.alloc_psum_tensor("PB2", [128, 1024], F32).ap()
        self.PSM = nc.alloc_psum_tensor("PSM", [128, 1024], F32).ap()
        self.PS1 = self.PSM[:, 0:512]
        self.PS2 = self.PSM[:, 512:1024]
        self.PS3 = nc.alloc_psum_tensor("PS3", [128, 512], F32).ap()
        self.PT = nc.alloc_psum_tensor("PT", [128, 1024], BF16).ap()
        assert nc.psum_banks_remaining == 0, nc.psum_banks_remaining

    def declare_sems(self):
        nc = self.nc
        mk = lambda n: Sem(nc, n)
        self.s_pe = mk("s_pe")
        self.s_act = mk("s_act")
        self.s_dve = mk("s_dve")
        self.s_misc = mk("s_misc")
        self.s_xb = mk("s_xb")
        self.s_qx = mk("s_qx")
        self.s_w1 = mk("s_w1")
        self.s_w2 = mk("s_w2")
        self.s_xf = [mk("s_xf0"), mk("s_xf1")]
        self.s_kst = [mk("s_kst%d" % i) for i in range(4)]
        self.s_kh = [mk("s_kh0"), mk("s_kh1")]
        self.s_vh = [mk("s_vh0"), mk("s_vh1")]
        self.s_mask = mk("s_mask")
        self.s_qxt = [mk("s_qxt0"), mk("s_qxt1")]
        self.s_x2d = mk("s_x2d")
        self.s_cwd = mk("s_cwd")
        self.s_wgu = [mk("s_wgu%d" % i) for i in range(4)]
        self.s_wd = [mk("s_wd0"), mk("s_wd1")]
        self.s_wsm = mk("s_wsm")
        self.s_cwr = [mk("s_cwr0"), mk("s_cwr1")]
        self.s_ain = [mk("s_ain0"), mk("s_ain1")]
        self.s_aout = [mk("s_aout0"), mk("s_aout1")]
        self.s_out = mk("s_out")

    def ph_consts(self):
        s = self.s_misc
        self.dma("sync", self.ident_sb[:, :], self.ident_bf[:, :], s)
        self.dma("sync", self.ident32_sb[:, :], self.ident32[:, :], s)
        self.dma("sync", self.ones_sb[:, :], self.ones_col[:, :], s)
        self.dma("sync", self.onesr32_sb[:, :], self.onesr32[:, :], s)
        self.dma("sync", self.onesrbf_sb[:, :], self.onesrbf[:, :], s)
        self.dma("sync", self.bq_sb[:, :], self.bq[:, :], s)
        self.dma("sync", self.bk_sb[:, :], self.bk[:, :], s)
        self.dma("sync", self.bv_sb[:, :], self.bv[:, :], s)
        self.consts_done = s.n
        self.dma("gpsimd",
                 self.gw_sb[:, :].rearrange("p (c e) -> p c e", c=NH),
                 self.gwt.rearrange("(c p) e -> p c e", p=128), self.s_w1)
        self.gw_done = self.s_w1.n

    def ph_casts(self):
        self.dma("gpsimd", self.xb_bf[0:512, :], self.xb[0:512, :], self.s_xb)
        self.dma("gpsimd",
                 self.W1[:, :].rearrange("p (c f) -> p c f", c=16),
                 self.Wk.rearrange("(c p) f -> p c f", p=128), self.s_w1)
        self.dma("gpsimd",
                 self.Wv_sb[:, :].rearrange("p (c f) -> p c f", c=16),
                 self.Wv.rearrange("(c p) f -> p c f", p=128), self.s_w2)
        for kt in range(1, 8):
            self.dma("gpsimd", self.xb_bf[kt * 512:(kt + 1) * 512, :],
                     self.xb[kt * 512:(kt + 1) * 512, :], self.s_xb)
        self.dma("gpsimd", self.qx_bf[:, :], self.qx[:, :], self.s_qx)

    def xf_transposes(self, src_dram, kt, slot, sem):
        for hc in range(16):
            self.dma("sync",
                     self.XF[slot][:, hc * 512:(hc + 1) * 512],
                     src_dram[kt * 512:(kt + 1) * 512, hc * 128:(hc + 1) * 128],
                     sem, transpose=True)

    def ph_A(self):
        slots = [self.PB1[:, 0:512], self.PB1[:, 512:1024],
                 self.PB2[:, 0:512], self.PB2[:, 512:1024],
                 self.PS1[:, :], self.PS2[:, :]]
        slot_free = [0] * 6
        kst_fill = [0] * 4
        si = 0
        ki = 0
        xf_fill = [0, 0]
        self.pe_a_done = {}
        self.wait("sync", self.s_xb, 16)
        self.xf_transposes(self.xb_bf, 0, 0, self.s_xf[0])
        self._xb_wait = {0: 16}
        xf_fill[0] = 1
        self.wait("tensor", self.s_misc, self.consts_done)
        self.wait("tensor", self.s_w1, self.s_w1.n)
        self.wait("tensor", self.s_w2, self.s_w2.n)
        for kt in range(8):
            sx = kt % 2
            if kt + 1 < 8:
                ns = (kt + 1) % 2
                if kt >= 1:
                    self.wait("sync", self.s_pe, self.pe_a_done[kt - 1])
                self.wait("sync", self.s_xb, 16 * (kt + 2))
                self.xf_transposes(self.xb_bf, kt + 1, ns, self.s_xf[ns])
                xf_fill[ns] += 1
            self.wait("tensor", self.s_xf[sx], 256 * xf_fill[sx])
            for wt, bias, dram in ((self.W1, self.bk_sb, self.K_dram),
                                   (self.Wv_sb, self.bv_sb, self.V_dram)):
                for fc in range(16):
                    ps = slots[si % 6]
                    self.wait("tensor", self.s_act, slot_free[si % 6])
                    pe_done = 0
                    for hc in range(16):
                        pe_done = self.pe(
                            lambda e, ps=ps, wt=wt, hc=hc, fc=fc, sx=sx:
                            e.matmul(ps, wt[:, hc * 2048 + fc * 128: hc * 2048 + (fc + 1) * 128],
                                     self.XF[sx][:, hc * 512:(hc + 1) * 512],
                                     start=(hc == 0), stop=(hc == 15), skip_group_check=True),
                            inc=(hc == 15))
                    ks = ki % 4
                    self.wait("scalar", self.s_pe, pe_done)
                    self.wait("scalar", self.s_kst[ks], 16 * kst_fill[ks])
                    a_tgt = self.act(lambda e, ps=ps, b=bias, fc=fc, ks=ks:
                                     e.activation(out=self.KST[ks][:, :], in_=ps, func=AF.Identity,
                                                  bias=b[:, fc:fc + 1]))
                    slot_free[si % 6] = a_tgt
                    self.wait("sync", self.s_act, a_tgt)
                    self.dma("sync", dram[fc][:, kt * 512:(kt + 1) * 512],
                             self.KST[ks][:, :], self.s_kst[ks])
                    kst_fill[ks] += 1
                    si += 1
                    ki += 1
            self.pe_a_done[kt] = self.s_pe.n

    def ph_A2(self):
        self.wait("gpsimd", self.s_pe, self.s_pe.n)
        self.dma("gpsimd", self.W1[:, :].rearrange("p (c f) -> p c f", c=16),
                 self.Wq.rearrange("(c p) f -> p c f", p=128), self.s_w1)
        wq_done = self.s_w1.n
        self.wait("sync", self.s_qx, 16)
        self.wait("sync", self.s_pe, self.s_pe.n)
        xf_tgt = [0, 0]
        for qt in range(2):
            self.xf_transposes(self.qx_bf, qt, qt, self.s_xf[qt])
            xf_tgt[qt] = self.s_xf[qt].n
        self.wait("tensor", self.s_w1, wq_done)
        slots = [self.PS1[:, :], self.PS2[:, :], self.PS3[:, :]]
        free_act = [0, 0, 0]
        si = 0
        for qt in range(2):
            self.wait("tensor", self.s_xf[qt], xf_tgt[qt])
            for fc in range(16):
                ps = slots[si % 3]
                self.wait("tensor", self.s_act, free_act[si % 3])
                pe_done = 0
                for hc in range(16):
                    pe_done = self.pe(
                        lambda e, ps=ps, hc=hc, fc=fc, qt=qt:
                        e.matmul(ps, self.W1[:, hc * 2048 + fc * 128: hc * 2048 + (fc + 1) * 128],
                                 self.XF[qt][:, hc * 512:(hc + 1) * 512],
                                 start=(hc == 0), stop=(hc == 15), skip_group_check=True),
                        inc=(hc == 15))
                self.wait("scalar", self.s_pe, pe_done)
                a_tgt = self.act(lambda e, ps=ps, fc=fc, qt=qt:
                                 e.activation(out=self.Q_sb[:, fc * 1024 + qt * 512:
                                                            fc * 1024 + qt * 512 + 512],
                                              in_=ps, func=AF.Identity,
                                              bias=self.bq_sb[:, fc:fc + 1]))
                free_act[si % 3] = a_tgt
                si += 1
        self.q_done_act = self.s_act.n

    def ph_B(self):
        PTf = self.PT.bitcast(F32)
        self.wait("sync", self.s_pe, self.s_pe.n)
        self.dma("sync", self.masks_sb[:, :].rearrange("p (k q) -> p k q", k=32),
                 self.masks.rearrange("k p q -> p k q"), self.s_mask)
        for st in self.s_kst:
            self.wait("sync", st, st.n)
        kh_fill = [0, 0]
        vt_fill = [0, 0]
        self.dma("sync", self.KH[0][:, :], self.K_dram[0][:, :], self.s_kh[0])
        kh_fill[0] = 1
        for kc in range(32):
            self.dma("sync", self.VT[0][:, kc * 128:(kc + 1) * 128],
                     self.V_dram[0][:, kc * 128:(kc + 1) * 128], self.s_vh[0],
                     transpose=True)
        vt_fill[0] = 1
        self.wait("tensor", self.s_mask, 16)
        self.wait("tensor", self.s_act, self.q_done_act)
        sc_slots = [self.PB2, self.PSM]
        sslot_free = [0, 0]
        exp_tgt = {}
        mask_tgt = {}
        ctx_free_dve = 0
        den_free_act = 0
        self.head_pe_done = {}
        for h in range(NH):
            sl = h % 2
            if h + 1 < NH:
                ns = (h + 1) % 2
                if h >= 1:
                    self.wait("sync", self.s_pe, self.head_pe_done[h - 1])
                self.dma("sync", self.KH[ns][:, :], self.K_dram[h + 1][:, :], self.s_kh[ns])
                kh_fill[ns] += 1
                for kc in range(32):
                    self.dma("sync", self.VT[ns][:, kc * 128:(kc + 1) * 128],
                             self.V_dram[h + 1][:, kc * 128:(kc + 1) * 128], self.s_vh[ns],
                             transpose=True)
                vt_fill[ns] += 1
            self.wait("tensor", self.s_kh[sl], 16 * kh_fill[sl])
            self.wait("tensor", self.s_vh[sl], 16 * 32 * vt_fill[sl])
            self.wait("tensor", self.s_dve, ctx_free_dve)
            self.wait("tensor", self.s_act, den_free_act)
            den_started = [False, False]
            ctx_started = [False, False]
            for it in range(34):
                if it < 32:
                    kc = it
                    slot = sc_slots[kc % 2]
                    qs = 128 * (kc // 4)
                    self.wait("tensor", self.s_act, sslot_free[kc % 2])
                    pe_cnt = 0
                    for (a0, w) in score_parts(kc):
                        pe_cnt = self.pe(lambda e, slot=slot, sl=sl, kc=kc, a0=a0, w=w, h=h:
                                         e.matmul(slot[:, a0:a0 + w],
                                                  self.KH[sl][:, kc * 128:(kc + 1) * 128],
                                                  self.Q_sb[:, h * 1024 + a0: h * 1024 + a0 + w],
                                                  start=True, stop=True, skip_group_check=True),
                                         inc=True)
                    self.wait("scalar", self.s_pe, pe_cnt)
                    a_tgt = self.act(lambda e, slot=slot, qs=qs, es=kc % 3:
                                     e.activation(out=self.EXP[:, es * 1024 + qs: (es + 1) * 1024],
                                                  in_=slot[:, qs:1024], func=AF.Exp,
                                                  scale=INV_SQRT_DH))
                    sslot_free[kc % 2] = a_tgt
                    exp_tgt[kc] = a_tgt
                    self.wait("vector", self.s_act, a_tgt)
                    mask_tgt[kc] = self.dve(
                        lambda e, es=kc % 3, qs=qs, kc=kc:
                        e.tensor_tensor(out=self.EXP[:, es * 1024 + qs: es * 1024 + qs + 128],
                                        in0=self.EXP[:, es * 1024 + qs: es * 1024 + qs + 128],
                                        in1=self.masks_sb[:, kc * 128:(kc + 1) * 128],
                                        op=ALU.mult))
                j = it - 2
                if 0 <= j < 32:
                    self.wait("tensor", self.s_dve, mask_tgt[j])
                    for (a0, w) in score_parts(j):
                        bank = 0 if a0 < 512 else 1
                        cstart = not ctx_started[bank]
                        ctx_started[bank] = True
                        dstart = not den_started[bank]
                        den_started[bank] = True
                        clast = (j == 15 and bank == 0) or (j == 31 and bank == 1)
                        self.pe(lambda e, j=j, a0=a0, w=w, es=j % 3, st=cstart, sp=clast, sl=sl:
                                e.matmul(self.PB1[:, a0:a0 + w],
                                         self.VT[sl][:, j * 128:(j + 1) * 128],
                                         self.EXP[:, es * 1024 + a0: es * 1024 + a0 + w],
                                         start=st, stop=sp, skip_group_check=True))
                        dps = self.PS3 if bank == 0 else PTf
                        da0 = a0 - bank * 512
                        self.pe(lambda e, dps=dps, da0=da0, w=w, es=j % 3, a0=a0, st=dstart, sp=clast:
                                e.matmul(dps[0:1, da0:da0 + w], self.ones_sb[:, 0:1],
                                         self.EXP[:, es * 1024 + a0: es * 1024 + a0 + w],
                                         start=st, stop=sp, skip_group_check=True),
                                inc=True)
            head_pe = self.s_pe.n
            self.head_pe_done[h] = head_pe
            self.wait("scalar", self.s_pe, head_pe)
            self.wait("scalar", self.s_dve, getattr(self, "prev_r_tgt", 0))
            for bank, dps in ((0, self.PS3), (1, PTf)):
                den_free_act = self.act(lambda e, dps=dps, bank=bank:
                                        e.activation(out=self.den_sb[0:1, bank * 512:(bank + 1) * 512],
                                                     in_=dps[0:1, 0:512], func=AF.Copy))
            self.wait("vector", self.s_act, den_free_act)
            self.wait("vector", self.s_pe, getattr(self, "prev_bc_pe", 0))
            r_tgt = self.dve(lambda e: e.reciprocal(out=self.recip_sb[0:1, :],
                                                    in_=self.den_sb[0:1, :]))
            self.wait("vector", self.s_dve, r_tgt)
            r_tgt = self.dve(lambda e: e.tensor_copy(out=self.recipbf_sb[0:1, :],
                                                     in_=self.recip_sb[0:1, :]))
            self.prev_r_tgt = r_tgt
            for half in range(2):
                self.wait("tensor", self.s_dve, r_tgt)
                self.wait("tensor", self.s_act, sslot_free[1])
                bc_pe = self.pe(lambda e, half=half:
                                e.matmul(self.PSM[:, half * 512:(half + 1) * 512],
                                         self.onesrbf_sb[0:1, :],
                                         self.recipbf_sb[0:1, half * 512:(half + 1) * 512],
                                         start=True, stop=True, skip_group_check=True),
                                inc=True)
                self.prev_bc_pe = bc_pe
                self.wait("scalar", self.s_pe, bc_pe)
                self.wait("scalar", self.s_dve, ctx_free_dve)
                rb_tgt = self.act(lambda e, half=half:
                                  e.activation(out=self.rb_sb[:, :],
                                               in_=self.PSM[:, half * 512:(half + 1) * 512],
                                               func=AF.Copy))
                sslot_free[1] = rb_tgt
                self.wait("vector", self.s_act, rb_tgt)
                ctx_free_dve = self.dve(lambda e, h=h, half=half:
                                        e.tensor_tensor(
                                            out=self.ctx_sb[:, h * 1024 + half * 512:
                                                            h * 1024 + half * 512 + 512],
                                            in0=self.PB1[:, half * 512:(half + 1) * 512],
                                            in1=self.rb_sb[:, :], op=ALU.mult))
        self.b_done_dve = self.s_dve.n
        self.b_done_act = self.s_act.n

    def emit_ln(self, res_ts, x2_ts, sq_ts):
        v = "vector"
        t1 = self.dve(lambda e, r=res_ts: e.reduce_sum(self.s1_sb[:, :], r, axis=AX.X))
        self.wait("scalar", self.s_dve, t1)
        a1 = self.act(lambda e, r=res_ts, sq=sq_ts:
                      e.activation(out=sq, in_=r, func=AF.Square,
                                   accum_out=self.s2_sb[:, :]))
        self.wait(v, self.s_act, a1)
        t2 = self.dve(lambda e: e.tensor_scalar(out=self.m_sb[:, :], in0=self.s1_sb[:, :],
                                                scalar1=1.0 / 2048.0, scalar2=None, op0=ALU.mult))
        self.wait(v, self.s_dve, t2)
        t3 = self.dve(lambda e: e.tensor_scalar(out=self.t_sb[:, :], in0=self.m_sb[:, :],
                                                scalar1=self.m_sb[:, 0:1], scalar2=None,
                                                op0=ALU.mult))
        t4 = self.dve(lambda e: e.tensor_scalar(out=self.v_sb[:, :], in0=self.s2_sb[:, :],
                                                scalar1=1.0 / 2048.0, scalar2=None, op0=ALU.mult))
        self.wait(v, self.s_dve, t4)
        t5 = self.dve(lambda e: e.tensor_tensor(out=self.v_sb[:, :], in0=self.v_sb[:, :],
                                                in1=self.t_sb[:, :], op=ALU.subtract))
        self.wait(v, self.s_dve, t5)
        t5b = self.dve(lambda e: e.tensor_scalar(out=self.v_sb[:, :], in0=self.v_sb[:, :],
                                                 scalar1=1e-5, scalar2=None, op0=ALU.add))
        self.wait("scalar", self.s_dve, t5b)
        a2 = self.act(lambda e: e.activation(out=self.t_sb[:, :], in_=self.v_sb[:, :],
                                             func=AF.Sqrt))
        self.wait(v, self.s_act, a2)
        t6 = self.dve(lambda e: e.reciprocal(out=self.rstd_sb[:, :], in_=self.t_sb[:, :]))
        self.wait(v, self.s_dve, t6)
        t7 = self.dve(lambda e, r=res_ts, o=x2_ts: e.tensor_scalar(
            out=o, in0=r, scalar1=self.m_sb[:, 0:1], scalar2=self.rstd_sb[:, 0:1],
            op0=ALU.subtract, op1=ALU.mult))
        return t7

    def ph_C(self):
        self.wait("gpsimd", self.s_pe, self.s_pe.n)
        self.dma("gpsimd", self.W1[:, :].rearrange("p (c f) -> p c f", c=16),
                 self.Wo.rearrange("(c p) f -> p c f", p=128), self.s_w1)
        wo_done = self.s_w1.n
        qxt_fill = [0, 0]
        qxt_fill_at = {}
        self.dma("sync", self.TS[0][:, :], self.qx[0:128, :], self.s_qxt[0])
        qxt_fill[0] = 1
        qxt_fill_at[0] = 1
        self.dma("sync", self.TS[1][:, :], self.qx[128:256, :], self.s_qxt[1])
        qxt_fill[1] = 1
        qxt_fill_at[1] = 1
        self.wait("tensor", self.s_w1, wo_done)
        self.wait("tensor", self.s_dve, self.b_done_dve)
        self.wait("tensor", self.s_act, self.b_done_act)
        wo_pe = {}
        res_dve = {}
        ln_x2 = {}
        sq_act = {}
        tb_act = {}
        tb_pe = {}
        lg_copied = {}
        cwtr_dve = {}
        x2d_n = {}
        pt_free = 0

        def res_slot(t):
            return self.TS[2] if t % 2 == 0 else self.TS[4]

        for it in range(9):
            tcr = it - 1
            if 0 <= tcr < 8:
                sl = tcr % 2
                self.wait("vector", self.s_pe, wo_pe[tcr])
                self.wait("vector", self.s_qxt[sl], 16 * qxt_fill_at[tcr])
                if tcr >= 2:
                    self.wait("vector", self.s_dve, ln_x2.get(tcr - 2, 0))
                for half in range(2):
                    ps = self.PB1 if half == 0 else self.PB2
                    self.dve(lambda e, ps=ps, sl=sl, half=half, tcr=tcr:
                             e.tensor_tensor(out=res_slot(tcr)[:, half * 1024:(half + 1) * 1024],
                                             in0=ps[:, :],
                                             in1=self.TS[sl][:, half * 1024:(half + 1) * 1024],
                                             op=ALU.add))
                res_dve[tcr] = self.s_dve.n
            if it < 8:
                tc = it
                if tc >= 1:
                    self.wait("tensor", self.s_dve, res_dve[tc - 1])
                pe_c = 0
                for hc in range(16):
                    for nb in range(4):
                        ps = self.PB1 if nb < 2 else self.PB2
                        off = (nb % 2) * 512
                        pe_c = self.pe(
                            lambda e, hc=hc, tc=tc, ps=ps, off=off, nb=nb:
                            e.matmul(ps[:, off:off + 512],
                                     self.ctx_sb[:, hc * 1024 + tc * 128: hc * 1024 + (tc + 1) * 128],
                                     self.W1[:, hc * 2048 + nb * 512: hc * 2048 + (nb + 1) * 512],
                                     start=(hc == 0), stop=(hc == 15), skip_group_check=True),
                            inc=(hc == 15 and nb == 3))
                wo_pe[tc] = pe_c
                self.last_wo_pe = pe_c
            if 0 <= tcr < 8:
                sl = tcr % 2
                self.wait("vector", self.s_dve, res_dve[tcr])
                if tcr >= 1:
                    self.wait("vector", self.s_x2d, x2d_n[tcr - 1])
                    self.wait("vector", self.s_act, tb_act.get(tcr - 1, 0))
                x2_t = self.emit_ln(res_slot(tcr)[:, :], self.TS[3][:, :], self.TS[sl][:, :])
                ln_x2[tcr] = x2_t
                sq_act[tcr] = self.s_act.n
                if tcr + 2 < 8:
                    self.wait("sync", self.s_act, sq_act[tcr])
                    self.dma("sync", self.TS[sl][:, :],
                             self.qx[(tcr + 2) * 128:(tcr + 3) * 128, :], self.s_qxt[sl])
                    qxt_fill[sl] += 1
                    qxt_fill_at[tcr + 2] = qxt_fill[sl]
                self.wait("sync", self.s_dve, x2_t)
                self.dma("sync", self.x2_dram[tcr * 128:(tcr + 1) * 128, :], self.TS[3][:, :],
                         self.s_x2d)
                x2d_n[tcr] = self.s_x2d.n
                self.wait("scalar", self.s_dve, x2_t)
                if tcr >= 1:
                    self.wait("scalar", self.s_pe, tb_pe.get(tcr - 1, 0))
                tb_t = self.act(lambda e: e.activation(out=self.TB[:, :], in_=self.TS[3][:, :],
                                                       func=AF.Copy))
                tb_act[tcr] = tb_t
                self.wait("tensor", self.s_act, tb_t)
                for grp in range(2):
                    self.wait("tensor", self.s_dve, pt_free)
                    pe_done = 0
                    for i in range(8):
                        hc = grp * 8 + i
                        pe_done = self.pe(lambda e, hc=hc, i=i:
                                          e.transpose(self.PT[:, i * 128:(i + 1) * 128],
                                                      self.TB[:, hc * 128:(hc + 1) * 128],
                                                      self.ident_sb[:, :]),
                                          inc=(i == 7))
                    self.wait("vector", self.s_pe, pe_done)
                    pt_free = self.dve(lambda e, g=grp, tc=tcr:
                                       e.tensor_copy(
                                           out=self.x2f_sb[:, :]
                                           .rearrange("p (c t) -> p c t", c=16)
                                           [:, g * 8:(g + 1) * 8, tc * 128:(tc + 1) * 128],
                                           in_=self.PT[:, :].rearrange("p (c t) -> p c t", c=8)))
                tb_pe[tcr] = self.s_pe.n
                x2f_dve = self.s_dve.n
                self.wait("tensor", self.s_dve, x2f_dve)
                self.wait("tensor", self.s_w1, self.gw_done)
                if tcr >= 1:
                    self.wait("tensor", self.s_dve, lg_copied.get(tcr - 1, 0))
                lg_pe = 0
                for hc in range(16):
                    lg_pe = self.pe(lambda e, hc=hc, tc=tcr:
                                    e.matmul(self.PS2[:, 0:NE],
                                             self.x2f_sb[:, hc * 1024 + tc * 128:
                                                         hc * 1024 + (tc + 1) * 128],
                                             self.gw_sb[:, hc * NE:(hc + 1) * NE],
                                             start=(hc == 0), stop=(hc == 15),
                                             skip_group_check=True),
                                    inc=(hc == 15))
                self.wait("vector", self.s_pe, lg_pe)
                v = "vector"

                def step(fn):
                    t = self.dve(fn)
                    self.wait(v, self.s_dve, t)
                    return t

                lgc = step(lambda e: e.tensor_copy(out=self.lg_sb[:, :], in_=self.PS2[:, 0:NE]))
                lg_copied[tcr] = lgc
                step(lambda e: e.tensor_copy(out=self.work_sb[:, :], in_=self.lg_sb[:, :]))
                for r in range(4):
                    step(lambda e, r=r: e.reduce_max(self.mpos_sb[:, r:r + 1],
                                                     self.work_sb[:, :], axis=AX.X))
                    if r < 3:
                        step(lambda e, r=r: e.tensor_scalar(
                            out=self.eq_sb[:, :], in0=self.work_sb[:, :],
                            scalar1=self.mpos_sb[:, r:r + 1], scalar2=None, op0=ALU.is_equal))
                        step(lambda e: e.scalar_tensor_tensor(
                            out=self.work_sb[:, :], in0=self.eq_sb[:, :], scalar=-1e30,
                            in1=self.work_sb[:, :], op0=ALU.mult, op1=ALU.add))
                step(lambda e: e.tensor_scalar(
                    out=self.msk_sb[:, :], in0=self.lg_sb[:, :],
                    scalar1=self.mpos_sb[:, 3:4], scalar2=None, op0=ALU.is_ge))
                ee_t = step(lambda e: e.tensor_scalar(
                    out=self.ex_sb[:, :], in0=self.lg_sb[:, :],
                    scalar1=self.mpos_sb[:, 0:1], scalar2=None, op0=ALU.subtract))
                self.wait("scalar", self.s_dve, ee_t)
                aex = self.act(lambda e: e.activation(out=self.ex_sb[:, :],
                                                      in_=self.ex_sb[:, :], func=AF.Exp))
                self.wait(v, self.s_act, aex)
                step(lambda e: e.tensor_tensor(out=self.wm_sb[:, :], in0=self.ex_sb[:, :],
                                               in1=self.msk_sb[:, :], op=ALU.mult))
                step(lambda e: e.reduce_sum(self.rsc_sb[:, 0:1], self.wm_sb[:, :], axis=AX.X))
                step(lambda e: e.reciprocal(out=self.rsc_sb[:, 1:2], in_=self.rsc_sb[:, 0:1]))
                step(lambda e: e.tensor_scalar(
                    out=self.cwt_sb[:, :], in0=self.wm_sb[:, :],
                    scalar1=self.rsc_sb[:, 1:2], scalar2=None, op0=ALU.mult))
                step(lambda e: e.tensor_tensor(
                    out=self.cwst_sb[:, :], in0=self.cwt_sb[:, 0:NE:2],
                    in1=self.cwt_sb[:, 1:NE:2], op=ALU.add))
                cw_t = step(lambda e: e.tensor_scalar(
                    out=self.cwst_sb[:, :], in0=self.cwst_sb[:, :], scalar1=0.05,
                    scalar2=None, op0=ALU.mult))
                self.wait("tensor", self.s_dve, cw_t)
                if tcr >= 1:
                    self.wait("tensor", self.s_dve, cwtr_dve.get(tcr - 1, 0))
                self.pe(lambda e: e.transpose(self.PS3[0:32, 0:128], self.cwt_sb[:, :],
                                              self.ident32_sb[:, :]))
                cwtr_pe = self.pe(lambda e: e.transpose(self.PS3[0:16, 128:256],
                                                        self.cwst_sb[:, :],
                                                        self.ident32_sb[:, :]), inc=True)
                self.wait(v, self.s_pe, cwtr_pe)
                self.dve(lambda e: e.tensor_copy(out=self.cwtr_sb[:, :], in_=self.PS3[0:32, 0:128]))
                ctr2 = self.dve(lambda e: e.tensor_copy(out=self.cwstr_sb[:, :],
                                                        in_=self.PS3[0:16, 128:256]))
                cwtr_dve[tcr] = ctr2
                self.wait("sync", self.s_dve, ctr2)
                self.dma("sync", self.cw_dram[:, tcr * 128:(tcr + 1) * 128], self.cwtr_sb[:, :],
                         self.s_cwd)
                self.dma("sync", self.cws_dram[:, tcr * 128:(tcr + 1) * 128], self.cwstr_sb[:, :],
                         self.s_cwd)
        self.c_done_pe = self.s_pe.n
        self.c_done_dve = self.s_dve.n
        self.c_done_act = self.s_act.n
        self.c_cwd_total = self.s_cwd.n
        self.x2d_total = self.s_x2d.n

    def ph_D(self):
        wgu_fill = [0] * 4
        wgu_free = [0] * 4
        wd_fill = [0, 0]
        wd_free = [0, 0]
        sm_free = 0
        cwr_fill = [0, 0]
        cwr_free = [0, 0]
        ain_fill = [0, 0]
        aout_fill = [0, 0]
        ein_free = [0, 0]
        eout_free = [0, 0]
        prev_eg_aout = {}
        g_free_act = 0
        u_free_dve = 0
        cw_ps3_free = 0
        cwb_free_dve = {}
        aeg_wait = 0
        down_free = [0, 0]

        self.wait("gpsimd", self.s_pe, self.c_done_pe)
        self.wait("gpsimd", self.s_dve, self.c_done_dve)
        self.wait("gpsimd", self.s_act, self.c_done_act)
        self.wait("gpsimd", self.s_x2d, self.x2d_total)
        self.wait("sync", self.s_cwd, self.c_cwd_total)
        self.wait("sync", self.s_x2d, self.x2d_total)
        self.wait("tensor", self.s_dve, self.c_done_dve)

        def load_wgu_half(W, e_glob, hf, slot):
            self.wait("gpsimd", self.s_pe, wgu_free[slot])
            self.dma("gpsimd", self.WGU[slot][:, :].rearrange("p (c i) -> p c i", c=16),
                     W[e_glob].rearrange("(c p) i -> p c i", p=128)
                     [:, :, hf * 384:(hf + 1) * 384], self.s_wgu[slot])
            wgu_fill[slot] += 1

        def load_wd(e_glob, slot):
            self.wait("gpsimd", self.s_pe, wd_free[slot])
            self.dma("gpsimd", self.WD[slot][:, :].rearrange("p (c f) -> p c f", c=6),
                     self.Wd[e_glob].rearrange("(c p) f -> p c f", p=128), self.s_wd[slot])
            wd_fill[slot] += 1

        def load_small(sc):
            self.wait("gpsimd", self.s_pe, sm_free)
            self.dma("gpsimd", self.SMG[:, :].rearrange("p (c i) -> p c i", c=16),
                     self.cWg[sc].rearrange("(c p) i -> p c i", p=128), self.s_wsm)
            self.dma("gpsimd", self.SMU[:, :].rearrange("p (c i) -> p c i", c=16),
                     self.cWu[sc].rearrange("(c p) i -> p c i", p=128), self.s_wsm)
            self.dma("gpsimd", self.SMD[:, :], self.cWd[sc][0:128, :], self.s_wsm)

        def load_cwr(src, row, tile, slot):
            self.wait("sync", self.s_pe, cwr_free[slot])
            self.dma("sync", self.CWR[slot][0:1, :],
                     src[row:row + 1, tile * 512:(tile + 1) * 512], self.s_cwr[slot])
            cwr_fill[slot] += 1

        cw_seq = []
        for eg in range(self.neg):
            for ei in range(2):
                for tile in range(2):
                    cw_seq.append((self.cw_dram, 2 * eg + ei, tile))
            for tile in range(2):
                cw_seq.append((self.cws_dram, eg, tile))
        cw_loaded = 0

        def ensure_cwr(idx):
            nonlocal cw_loaded
            while cw_loaded <= idx + 1 and cw_loaded < len(cw_seq):
                src, row, tile = cw_seq[cw_loaded]
                load_cwr(src, row, tile, cw_loaded % 2)
                cw_loaded += 1

        def bcast_cw(idx, half):
            nonlocal cw_ps3_free
            slot = idx % 2
            ensure_cwr(idx)
            self.wait("tensor", self.s_cwr[slot], 16 * (idx // 2 + 1))
            self.wait("tensor", self.s_act, cw_ps3_free)
            self.wait("tensor", self.s_dve, cwb_free_dve.get(half, 0))
            bc_pe = self.pe(lambda e, slot=slot:
                            e.matmul(self.PS3[:, :], self.onesrbf_sb[0:1, :],
                                     self.CWR[slot][0:1, :], start=True, stop=True,
                                     skip_group_check=True), inc=True)
            cwr_free[slot] = bc_pe
            self.wait("scalar", self.s_pe, bc_pe)
            cw_ps3_free = self.act(lambda e, half=half:
                                   e.activation(out=self.cwb_sb[:, half * 512:(half + 1) * 512],
                                                in_=self.PS3[:, :], func=AF.Copy))
            return cw_ps3_free

        load_wgu_half(self.Wg, 0, 0, 0)
        load_wgu_half(self.Wu, 0, 0, 1)
        load_wgu_half(self.Wg, 0, 1, 2)
        load_wgu_half(self.Wu, 0, 1, 3)
        load_wd(0, 0)
        load_wd(1, 1)
        load_small(0)
        first_unit = True
        d_last_pe = 0
        for eg in range(self.neg):
            for ei in range(2):
                e_glob = 2 * eg + ei
                seq_base = 6 * eg + 2 * ei
                cw_acts = [bcast_cw(seq_base + 0, 0), bcast_cw(seq_base + 1, 1)]
                nxt = e_glob + 1
                if nxt < NE:
                    pass
                for fc in range(6):
                    hf = fc // 3
                    gslot = 2 * hf
                    uslot = 2 * hf + 1
                    self.wait("tensor", self.s_wgu[gslot], 16 * wgu_fill[gslot])
                    self.wait("tensor", self.s_wgu[uslot], 16 * wgu_fill[uslot])
                    for tile in range(2):
                        self.wait("tensor", self.s_act, g_free_act)
                        g_pe = 0
                        for hc in range(16):
                            g_pe = self.pe(
                                lambda e, hc=hc, fc=fc, hf=hf, gslot=gslot, tile=tile:
                                e.matmul(self.PS1[:, :],
                                         self.WGU[gslot][:, hc * 384 + (fc % 3) * 128:
                                                         hc * 384 + (fc % 3 + 1) * 128],
                                         self.x2f_sb[:, hc * 1024 + tile * 512:
                                                     hc * 1024 + (tile + 1) * 512],
                                         start=(hc == 0), stop=(hc == 15),
                                         skip_group_check=True),
                                inc=(hc == 15))
                        self.wait("scalar", self.s_pe, g_pe)
                        g_free_act = self.act(lambda e: e.activation(
                            out=self.SG[:, :], in_=self.PS1[:, :], func=AF.Silu))
                        self.wait("tensor", self.s_dve, u_free_dve)
                        u_pe = 0
                        for hc in range(16):
                            u_pe = self.pe(
                                lambda e, hc=hc, fc=fc, hf=hf, uslot=uslot, tile=tile:
                                e.matmul(self.PS2[:, :],
                                         self.WGU[uslot][:, hc * 384 + (fc % 3) * 128:
                                                         hc * 384 + (fc % 3 + 1) * 128],
                                         self.x2f_sb[:, hc * 1024 + tile * 512:
                                                     hc * 1024 + (tile + 1) * 512],
                                         start=(hc == 0), stop=(hc == 15),
                                         skip_group_check=True),
                                inc=(hc == 15))
                        self.wait("vector", self.s_pe, u_pe)
                        self.wait("vector", self.s_act, g_free_act)
                        self.wait("vector", self.s_act, cw_acts[tile])
                        t1 = self.dve(lambda e: e.tensor_tensor(
                            out=self.T1[:, :], in0=self.SG[:, :], in1=self.PS2[:, :],
                            op=ALU.mult))
                        u_free_dve = t1
                        self.wait("vector", self.s_dve, t1)
                        t2 = self.dve(lambda e, ei=ei, fc=fc, tile=tile: e.tensor_tensor(
                            out=self.AEG[:, (ei * 6 + fc) * 1024 + tile * 512:
                                         (ei * 6 + fc) * 1024 + (tile + 1) * 512],
                            in0=self.T1[:, :], in1=self.cwb_sb[:, tile * 512:(tile + 1) * 512],
                            op=ALU.mult))
                        cwb_free_dve[tile] = t2
                    if fc % 3 == 2:
                        wgu_free[gslot] = self.s_pe.n
                        wgu_free[uslot] = self.s_pe.n
                        if e_glob + 1 < 2 * self.neg:
                            load_wgu_half(self.Wg, e_glob + 1, hf, gslot)
                            load_wgu_half(self.Wu, e_glob + 1, hf, uslot)
            sc = eg
            seq_sc = 6 * eg + 4
            cw_acts = [bcast_cw(seq_sc + 0, 0), bcast_cw(seq_sc + 1, 1)]
            self.wait("tensor", self.s_wsm, 16 * 3 * (eg + 1))
            for tile in range(2):
                self.wait("tensor", self.s_act, g_free_act)
                g_pe = 0
                for hc in range(16):
                    g_pe = self.pe(lambda e, hc=hc, tile=tile:
                                   e.matmul(self.PS1[:, :],
                                            self.SMG[:, hc * 128:(hc + 1) * 128],
                                            self.x2f_sb[:, hc * 1024 + tile * 512:
                                                        hc * 1024 + (tile + 1) * 512],
                                            start=(hc == 0), stop=(hc == 15),
                                            skip_group_check=True),
                                   inc=(hc == 15))
                self.wait("scalar", self.s_pe, g_pe)
                g_free_act = self.act(lambda e: e.activation(
                    out=self.SG[:, :], in_=self.PS1[:, :], func=AF.Silu))
                self.wait("tensor", self.s_dve, u_free_dve)
                u_pe = 0
                for hc in range(16):
                    u_pe = self.pe(lambda e, hc=hc, tile=tile:
                                   e.matmul(self.PS2[:, :],
                                            self.SMU[:, hc * 128:(hc + 1) * 128],
                                            self.x2f_sb[:, hc * 1024 + tile * 512:
                                                        hc * 1024 + (tile + 1) * 512],
                                            start=(hc == 0), stop=(hc == 15),
                                            skip_group_check=True),
                                   inc=(hc == 15))
                self.wait("vector", self.s_pe, u_pe)
                self.wait("vector", self.s_act, g_free_act)
                self.wait("vector", self.s_act, cw_acts[tile])
                t1 = self.dve(lambda e: e.tensor_tensor(
                    out=self.T1[:, :], in0=self.SG[:, :], in1=self.PS2[:, :], op=ALU.mult))
                u_free_dve = t1
                self.wait("vector", self.s_dve, t1)
                t2 = self.dve(lambda e, tile=tile: e.tensor_tensor(
                    out=self.ASM[:, tile * 512:(tile + 1) * 512],
                    in0=self.T1[:, :], in1=self.cwb_sb[:, tile * 512:(tile + 1) * 512],
                    op=ALU.mult))
                cwb_free_dve[tile] = t2
            aeg_wait = self.s_dve.n
            self.wait("tensor", self.s_wd[0], 16 * wd_fill[0])
            self.wait("tensor", self.s_wd[1], 16 * wd_fill[1])
            self.wait("tensor", self.s_dve, aeg_wait)
            ui = 0
            for tile in range(2):
                for st in range(4):
                    r0 = tile * 512 + st * 128
                    if eg > 0:
                        cnt0, cnt1 = prev_eg_aout[(tile, st)]
                        for fh in range(2):
                            esl = fh
                            self.wait("sync", self.s_dve, ein_free[esl])
                            self.wait("sync", self.s_aout[0], 16 * cnt0)
                            self.wait("sync", self.s_aout[1], 16 * cnt1)
                            self.dma("sync", self.EIN[esl][:, :],
                                     self.acc_dram[r0:r0 + 128, fh * 1024:(fh + 1) * 1024],
                                     self.s_ain[esl])
                            ain_fill[esl] += 1
                    for fh in range(2):
                        pb = self.PB1 if fh == 0 else self.PB2
                        pbi = fh
                        esl = fh
                        self.wait("tensor", self.s_dve, down_free[pbi])
                        dn_pe = 0
                        nmm = 0
                        for ei in range(2):
                            for ic in range(6):
                                for nb in range(2):
                                    nmm += 1
                                    dn_pe = self.pe(
                                        lambda e, ei=ei, ic=ic, nb=nb, fh=fh, pb=pb,
                                        tile=tile, st=st, first=(ei == 0 and ic == 0):
                                        e.matmul(pb[:, nb * 512:(nb + 1) * 512],
                                                 self.AEG[:, (ei * 6 + ic) * 1024 + tile * 512
                                                          + st * 128: (ei * 6 + ic) * 1024
                                                          + tile * 512 + (st + 1) * 128],
                                                 self.WD[ei][:, ic * 2048 + fh * 1024
                                                             + nb * 512: ic * 2048 + fh * 1024
                                                             + (nb + 1) * 512],
                                                 start=first, stop=False,
                                                 skip_group_check=True),
                                        inc=False)
                        for nb in range(2):
                            dn_pe = self.pe(
                                lambda e, nb=nb, fh=fh, pb=pb, tile=tile, st=st:
                                e.matmul(pb[:, nb * 512:(nb + 1) * 512],
                                         self.ASM[:, tile * 512 + st * 128:
                                                  tile * 512 + (st + 1) * 128],
                                         self.SMD[:, fh * 1024 + nb * 512:
                                                  fh * 1024 + (nb + 1) * 512],
                                         start=False, stop=True, skip_group_check=True),
                                inc=(nb == 1))
                        self.wait("vector", self.s_pe, dn_pe)
                        osl = fh
                        self.wait("vector", self.s_aout[osl], 16 * aout_fill[osl])
                        if eg > 0:
                            self.wait("vector", self.s_ain[esl], 16 * ain_fill[esl])
                            dv = self.dve(lambda e, pb=pb, osl=osl, esl=esl:
                                          e.tensor_tensor(out=self.EOUT[osl][:, :],
                                                          in0=pb[:, :],
                                                          in1=self.EIN[esl][:, :],
                                                          op=ALU.add))
                        else:
                            dv = self.dve(lambda e, pb=pb, osl=osl:
                                          e.tensor_copy(out=self.EOUT[osl][:, :], in_=pb[:, :]))
                        down_free[pbi] = dv
                        ein_free[esl] = dv
                        self.wait("sync", self.s_dve, dv)
                        self.dma("sync", self.acc_dram[r0:r0 + 128, fh * 1024:(fh + 1) * 1024],
                                 self.EOUT[osl][:, :], self.s_aout[osl])
                        aout_fill[osl] += 1
                    prev_eg_aout[(tile, st)] = (aout_fill[0], aout_fill[1])
                    ui += 1
            d_last_pe = self.s_pe.n
            wd_free[0] = d_last_pe
            wd_free[1] = d_last_pe
            sm_free = d_last_pe
            if eg + 1 < self.neg:
                load_wd(2 * (eg + 1), 0)
                load_wd(2 * (eg + 1) + 1, 1)
                load_small(eg + 1)
        self.d_done_pe = self.s_pe.n
        self.d_done_dve = self.s_dve.n
        self.d_aout_total = [self.s_aout[0].n, self.s_aout[1].n]

    def ph_E(self):
        self.wait("sync", self.s_aout[0], self.d_aout_total[0])
        self.wait("sync", self.s_aout[1], self.d_aout_total[1])
        self.wait("sync", self.s_pe, self.d_done_pe)
        self.wait("sync", self.s_dve, self.d_done_dve)
        acc_ready = {}
        x2_ready = {}
        acc_free = {0: 0, 1: 0}
        x2_free = {0: 0, 1: 0}
        res_free = 0
        out_n = {}
        ln_done = {}

        def acc_slots(t):
            if t % 2 == 0:
                return [(self.EIN[0][:, :], self.s_ain[0]), (self.EIN[1][:, :], self.s_ain[1])]
            return [(self.TS[4][:, 0:1024], self.s_ain[0]), (self.TS[4][:, 1024:2048], self.s_ain[1])]

        def load_tiles(t):
            par = t % 2
            self.wait("sync", self.s_dve, acc_free[par])
            for fh, (ap, sem) in enumerate(acc_slots(t)):
                self.dma("sync", ap,
                         self.acc_dram[t * 128:(t + 1) * 128, fh * 1024:(fh + 1) * 1024], sem)
            acc_ready[t] = (self.s_ain[0].n, self.s_ain[1].n)
            self.wait("sync", self.s_act, x2_free[par])
            self.dma("sync", self.TS[par][:, :], self.x2_dram[t * 128:(t + 1) * 128, :],
                     self.s_qxt[par])
            x2_ready[t] = self.s_qxt[par].n

        load_tiles(0)
        load_tiles(1)
        for tc in range(8):
            par = tc % 2
            self.wait("vector", self.s_ain[0], acc_ready[tc][0])
            self.wait("vector", self.s_ain[1], acc_ready[tc][1])
            self.wait("vector", self.s_qxt[par], x2_ready[tc])
            self.wait("vector", self.s_dve, res_free)
            slots = acc_slots(tc)
            for half in range(2):
                self.dve(lambda e, par=par, half=half, ap=slots[half][0]:
                         e.tensor_tensor(out=self.TS[2][:, half * 1024:(half + 1) * 1024],
                                         in0=ap,
                                         in1=self.TS[par][:, half * 1024:(half + 1) * 1024],
                                         op=ALU.add))
            r2 = self.s_dve.n
            acc_free[par] = r2
            self.wait("vector", self.s_dve, r2)
            if tc >= 1:
                self.wait("vector", self.s_out, out_n[tc - 1])
            o_t = self.emit_ln(self.TS[2][:, :], self.TS[3][:, :], self.TS[par][:, :])
            ln_done[tc] = o_t
            res_free = o_t
            x2_free[par] = self.s_act.n
            if tc + 2 < 8:
                load_tiles(tc + 2)
            self.wait("sync", self.s_dve, o_t)
            self.dma("sync", self.out[tc * 128:(tc + 1) * 128, :], self.TS[3][:, :], self.s_out)
            out_n[tc] = self.s_out.n
        if self.debug:
            self.wait("sync", self.s_pe, self.s_pe.n)
            self.dma("sync", self.d_K[:, :, :], self.K_dram[:, :, :], self.s_out)
            self.dma("sync", self.d_V[:, :, :], self.V_dram[:, :, :], self.s_out)
            self.dma("sync", self.d_x2[:, :], self.x2_dram[:, :], self.s_out)
            self.dma("sync", self.d_cw[:, :], self.cw_dram[:, :], self.s_out)
            self.dma("sync", self.d_cws[:, :], self.cws_dram[:, :], self.s_out)
            self.dma("sync", self.d_acc[:, :], self.acc_dram[:, :], self.s_out)
            self.dma("sync", self.d_aeg[:, :], self.AEG[:, :], self.s_out)
            self.dma("sync", self.d_asm[:, :], self.ASM[:, :], self.s_out)
            self.dma("sync", self.d_cwb[:, :], self.cwb_sb[:, :], self.s_out)
        self.wait("sync", self.s_out, self.s_out.n)

    def ph_debug_early(self):
        if not self.debug:
            return
        self.wait("sync", self.s_act, self.q_done_act)
        self.dma("sync", self.d_Q[:, :], self.Q_sb[:, :], self.s_out)
        self.wait("sync", self.s_dve, self.b_done_dve)
        self.dma("sync", self.d_ctx[:, :], self.ctx_sb[:, :], self.s_out)

    def build(self):
        self.ph_consts()
        self.ph_casts()
        self.ph_A()
        self.ph_A2()
        self.ph_B()
        if self.debug:
            self.ph_debug_early()
        self.ph_C()
        self.ph_D()
        self.ph_E()
        with self.nc.Block() as block:
            for name, reg in (("sync", block.sync), ("gpsimd", block.gpsimd),
                              ("vector", block.vector), ("scalar", block.scalar),
                              ("tensor", block.tensor)):
                fns = self.ops[name]

                def runner(e, fns=fns):
                    for f in fns:
                        f(e)
                reg(runner)
        return self.nc


def build_program(debug=0, neg=NEG):
    return Builder(debug=debug, neg=neg).build()


def stage_inputs(inputs):
    x = np.asarray(inputs["L0_attn_hidden_states"])
    ident_bf = np.eye(128, dtype=bfloat16)
    ident32 = np.eye(128, dtype=np.float32)
    ones_col = np.ones((128, 1), dtype=bfloat16)
    onesr32 = np.ones((1, 128), dtype=np.float32)
    shared = {
        "Wq": np.asarray(inputs["L0_attn_Wq"]),
        "Wk": np.asarray(inputs["L0_attn_Wk"]),
        "Wv": np.asarray(inputs["L0_attn_Wv"]),
        "Wo": np.asarray(inputs["L0_attn_Wo"]),
        "bq": np.ascontiguousarray(np.asarray(inputs["L0_attn_bq"]).reshape(16, 128).T),
        "bk": np.ascontiguousarray(np.asarray(inputs["L0_attn_bk"]).reshape(16, 128).T),
        "bv": np.ascontiguousarray(np.asarray(inputs["L0_attn_bv"]).reshape(16, 128).T),
        "gwt": np.ascontiguousarray(np.asarray(inputs["L0_ffn_gate_w"]).T),
        "Wg": np.asarray(inputs["L0_ffn_Wg"]),
        "Wu": np.asarray(inputs["L0_ffn_Wu"]),
        "Wd": np.asarray(inputs["L0_ffn_Wd"]),
        "cWg": np.asarray(inputs["L0_ffn_cWg"]),
        "cWu": np.asarray(inputs["L0_ffn_cWu"]),
        "cWd": np.asarray(inputs["L0_ffn_cWd"]),
        "ident_bf": ident_bf,
        "ident32": ident32,
        "ones_col": ones_col,
        "onesr32": onesr32,
        "onesrbf": np.ones((1, 128), dtype=bfloat16),
    }
    in_maps = []
    chunk_maps = []
    tri = np.triu(np.ones((128, 128), dtype=np.float32))
    for c in range(NCORE := 8):
        b = c // 4
        g = c % 4
        chunks = chunk_assign(g)
        chunk_maps.append(chunks)
        xb = np.ascontiguousarray(x[b])
        qx = np.ascontiguousarray(
            np.concatenate([x[b, 128 * ci:128 * (ci + 1)] for ci in chunks], axis=0))
        masks = np.zeros((32, 128, 128), dtype=np.float32)
        for kc in range(32):
            j = kc // 4
            p = chunks[j] + 1
            if kc < p - 1:
                masks[kc] = 1.0
            elif kc == p - 1:
                masks[kc] = tri
        m = {"xb": xb, "qx": qx, "masks": masks.astype(bfloat16)}
        m.update(shared)
        in_maps.append(m)
    return in_maps, chunk_maps


def kernel(**inputs):
    in_maps, chunk_maps = stage_inputs(inputs)
    nc = build_program(debug=0)
    res = run_bass_kernel_spmd(nc, in_maps, core_ids=list(range(8)))
    out = np.zeros((2, 4096, 2048), dtype=np.float32)
    for c in range(8):
        b = c // 4
        shard = res.results[c]["out"]
        for j, ci in enumerate(chunk_maps[c]):
            out[b, 128 * ci:128 * (ci + 1), :] = shard[128 * j:128 * (j + 1), :]
    return out


# version 08 (final)
# speedup vs baseline: 1.0037x; 1.0037x over previous
import numpy as np
import ml_dtypes
import concourse.bass as bass
import concourse.mybir as mybir
from concourse.bass_utils import run_bass_kernel_spmd

F32 = mybir.dt.float32
BF16 = mybir.dt.bfloat16
AF = mybir.ActivationFunctionType
ALU = mybir.AluOpType
AX = mybir.AxisListType
bfloat16 = ml_dtypes.bfloat16

HD = 2048
NH = 16
S = 4096
TPC = 1024
NE = 32
NSC = 16
IE = 768
NEG = 16
INV_SQRT_DH = float(1.0 / np.sqrt(128.0))
KB = 1024


def chunk_assign(g):
    return [4 * j + ((g + j) % 4) for j in range(8)]


def score_parts(kc):
    qs = 128 * (kc // 4)
    if qs < 512:
        return [(qs, 512 - qs), (512, 512)]
    return [(qs, 1024 - qs)]


class Sem:
    def __init__(self, nc, name):
        self.h = nc.alloc_semaphore(name)
        self.n = 0

    def inc(self, k=1):
        self.n += k
        return self.n


class Builder:
    def __init__(self, debug=0, neg=NEG):
        self.debug = debug
        self.neg = neg
        nc = bass.Bass("TRN2", target_bir_lowering=False, debug=False)
        self.nc = nc
        self.ops = {k: [] for k in ("sync", "gpsimd", "vector", "scalar", "tensor")}
        self.declare_io()
        self.declare_sbuf()
        self.declare_psum()
        self.declare_sems()

    def emit(self, eng, fn):
        self.ops[eng].append(fn)

    def wait(self, eng, sem, val):
        if val > 0:
            self.emit(eng, lambda e, s=sem.h, v=val: e.wait_ge(s, v))

    def dma(self, eng, out, in_, sem, **kw):
        sem.inc(16)
        self.emit(eng, lambda e, o=out, i=in_, s=sem.h, k=kw:
                  e.dma_start(out=o, in_=i, **k).then_inc(s, 16))

    def pe(self, fn, inc=False):
        if inc:
            self.s_pe.inc()
            self.emit("tensor", lambda e, f=fn: f(e).then_inc(self.s_pe.h, 1))
        else:
            self.emit("tensor", lambda e, f=fn: f(e))
        return self.s_pe.n

    def act(self, fn):
        self.s_act.inc()
        self.emit("scalar", lambda e, f=fn: f(e).then_inc(self.s_act.h, 1))
        return self.s_act.n

    def dve(self, fn):
        self.s_dve.inc()
        self.emit("vector", lambda e, f=fn: f(e).then_inc(self.s_dve.h, 1))
        return self.s_dve.n

    def declare_io(self):
        nc = self.nc
        D = nc.dram_tensor
        self.xb = D("xb", [S, HD], F32, kind="ExternalInput").ap()
        self.qx = D("qx", [TPC, HD], F32, kind="ExternalInput").ap()
        self.masks = D("masks", [32, 128, 128], BF16, kind="ExternalInput").ap()
        self.Wq = D("Wq", [HD, HD], F32, kind="ExternalInput").ap()
        self.Wk = D("Wk", [HD, HD], F32, kind="ExternalInput").ap()
        self.Wv = D("Wv", [HD, HD], F32, kind="ExternalInput").ap()
        self.Wo = D("Wo", [HD, HD], F32, kind="ExternalInput").ap()
        self.bq = D("bq", [128, 16], F32, kind="ExternalInput").ap()
        self.bk = D("bk", [128, 16], F32, kind="ExternalInput").ap()
        self.bv = D("bv", [128, 16], F32, kind="ExternalInput").ap()
        self.bv_row = D("bv_row", [1, HD], F32, kind="ExternalInput").ap()
        self.gwt = D("gwt", [HD, NE], F32, kind="ExternalInput").ap()
        self.Wg = D("Wg", [NE, HD, IE], F32, kind="ExternalInput").ap()
        self.Wu = D("Wu", [NE, HD, IE], F32, kind="ExternalInput").ap()
        self.Wd = D("Wd", [NE, IE, HD], F32, kind="ExternalInput").ap()
        self.cWg = D("cWg", [NSC, HD, 128], F32, kind="ExternalInput").ap()
        self.cWu = D("cWu", [NSC, HD, 128], F32, kind="ExternalInput").ap()
        self.cWd = D("cWd", [NSC, 128, HD], F32, kind="ExternalInput").ap()
        self.ident_bf = D("ident_bf", [128, 128], BF16, kind="ExternalInput").ap()
        self.ident32 = D("ident32", [128, 128], F32, kind="ExternalInput").ap()
        self.ones_col = D("ones_col", [128, 1], BF16, kind="ExternalInput").ap()
        self.onesr32 = D("onesr32", [1, 128], F32, kind="ExternalInput").ap()
        self.onesrbf = D("onesrbf", [1, 128], BF16, kind="ExternalInput").ap()

        self.out = D("out", [TPC, HD], F32, kind="ExternalOutput").ap()

        self.xb_bf = D("xb_bf", [S, HD], BF16, kind="Internal").ap()
        self.qx_bf = D("qx_bf", [TPC, HD], BF16, kind="Internal").ap()
        self.K_dram = D("K_dram", [NH, 128, S], BF16, kind="Internal").ap()
        self.V_dram = D("V_dram", [S, HD], BF16, kind="Internal").ap()
        self.x2_dram = D("x2_dram", [TPC, HD], F32, kind="Internal").ap()
        self.cw_dram = D("cw_dram", [NE, TPC], BF16, kind="Internal").ap()
        self.cws_dram = D("cws_dram", [NSC, TPC], BF16, kind="Internal").ap()
        self.acc_dram = D("acc_dram", [TPC, HD], F32, kind="Internal").ap()

        if self.debug:
            self.d_K = D("d_K", [NH, 128, S], BF16, kind="ExternalOutput").ap()
            self.d_V = D("d_V", [S, HD], BF16, kind="ExternalOutput").ap()
            self.d_Q = D("d_Q", [128, NH * TPC], BF16, kind="ExternalOutput").ap()
            self.d_ctx = D("d_ctx", [128, NH * TPC], BF16, kind="ExternalOutput").ap()
            self.d_x2 = D("d_x2", [TPC, HD], F32, kind="ExternalOutput").ap()
            self.d_cw = D("d_cw", [NE, TPC], BF16, kind="ExternalOutput").ap()
            self.d_cws = D("d_cws", [NSC, TPC], BF16, kind="ExternalOutput").ap()
            self.d_acc = D("d_acc", [TPC, HD], F32, kind="ExternalOutput").ap()
            self.d_aeg = D("d_aeg", [128, 12 * 1024], BF16, kind="ExternalOutput").ap()
            self.d_asm = D("d_asm", [128, 1024], BF16, kind="ExternalOutput").ap()
            self.d_cwb = D("d_cwb", [128, 1024], F32, kind="ExternalOutput").ap()

    def declare_sbuf(self):
        nc = self.nc
        A = lambda *a: nc.alloc_sbuf_tensor(*a).ap()
        self.ident_sb = A("ident_sb", [128, 128], BF16)
        self.ident32_sb = A("ident32_sb", [128, 128], F32)
        self.ones_sb = A("ones_sb", [128, 1], BF16)
        self.onesr32_sb = A("onesr32_sb", [1, 128], F32)
        self.onesrbf_sb = A("onesrbf_sb", [1, 128], BF16)
        self.bq_sb = A("bq_sb", [128, 16], F32)
        self.bk_sb = A("bk_sb", [128, 16], F32)
        self.bv_sb = A("bv_sb", [128, 16], F32)
        self.gw_sb = A("gw_sb", [128, NH * NE], BF16)

        base = nc._sbuf_addr_for_side("left")
        base = (base + 127) & ~127

        def At(name, shape, dtype, off_kb):
            return nc.alloc_sbuf_tensor_at(name, shape, dtype,
                                           offset=base + int(off_kb * KB)).ap()

        self.XF = [At("XF0", [128, 16 * 512], BF16, 0), At("XF1", [128, 16 * 512], BF16, 16)]
        self.masks_sb = At("masks_sb", [128, 32 * 128], BF16, 0)
        self.EXP = At("EXP", [128, 3 * 1024], BF16, 8)
        self.den_sb = At("den_sb", [1, 1024], F32, 14)
        self.recip_sb = At("recip_sb", [1, 1024], F32, 18)
        self.rb_sb = At("rb_sb", [128, 512], F32, 22)
        self.recipbf_sb = At("recipbf_sb", [1, 1024], BF16, 24)
        self.x2f_sb = At("x2f_sb", [128, 16 * 1024], BF16, 0)
        self.Wv_sb = At("Wv_sb", [128, 16 * 2048], BF16, 32)
        self.Q_sb = At("Q_sb", [128, NH * 1024], BF16, 32)
        self.ctx_sb = At("ctx_sb", [128, NH * 1024], BF16, 64)
        self.lg_sb = At("lg_sb", [128, NE], F32, 32)
        self.work_sb = At("work_sb", [128, NE], F32, 32.25)
        self.eq_sb = At("eq_sb", [128, NE], F32, 32.5)
        self.msk_sb = At("msk_sb", [128, NE], F32, 32.75)
        self.ex_sb = At("ex_sb", [128, NE], F32, 33.0)
        self.wm_sb = At("wm_sb", [128, NE], F32, 33.25)
        self.mpos_sb = At("mpos_sb", [128, 8], F32, 33.5)
        self.rsc_sb = At("rsc_sb", [128, 8], F32, 33.75)
        self.cwt_sb = At("cwt_sb", [128, NE], F32, 34.0)
        self.cwst_sb = At("cwst_sb", [128, NSC], F32, 34.25)
        self.cwtr_sb = At("cwtr_sb", [32, 128], BF16, 34.5)
        self.cwstr_sb = At("cwstr_sb", [16, 128], BF16, 35.0)
        self.bvrow_sb = At("bvrow_sb", [1, HD], BF16, 35.5)
        self.bvtile_sb = At("bvtile_sb", [128, 2048], F32, 180)
        self.bvrow32_sb = At("bvrow32_sb", [1, HD], F32, 188)
        self.AEG = At("AEG", [128, 12 * 1024], BF16, 32)
        self.ASM = At("ASM", [128, 1024], BF16, 56)
        self.SG = At("SG", [128, 512], F32, 58)
        self.T1 = At("T1", [128, 512], F32, 60)
        self.cwb_sb = At("cwb_sb", [128, 1024], F32, 62)
        self.EIN = [At("EIN0", [128, 1024], F32, 66), At("EIN1", [128, 1024], F32, 70)]
        self.EOUT = [At("EOUT0", [128, 1024], F32, 74), At("EOUT1", [128, 1024], F32, 78)]
        self.W1 = At("W1", [128, 16 * 2048], BF16, 96)
        self.KH = [At("KH0", [128, S], BF16, 96), At("KH1", [128, S], BF16, 104)]
        self.VT = [At("VT0", [128, S], BF16, 128), At("VT1", [128, S], BF16, 136)]
        self.WGU = [At("WGU%d" % i, [128, 16 * 384], BF16, 96 + 12 * i) for i in range(4)]
        self.WD = [At("WD0", [128, 6 * 2048], BF16, 144), At("WD1", [128, 6 * 2048], BF16, 168)]
        self.KST = [At("KST%d" % i, [128, 512], BF16, 160 + i) for i in range(4)]
        self.XSTR = [At("XSTR%d" % i, [128, 2048], BF16, 164 + 4 * i) for i in range(2)]
        self.VST = [At("VST%d" % i, [128, 2048], BF16, 172 + 4 * i) for i in range(2)]
        self.TS = [At("TS%d" % i, [128, 2048], F32, 160 + 8 * i) for i in range(4)]
        self.TS.append(At("TS4", [128, 2048], F32, 196))
        self.TB = At("TB", [128, 2048], BF16, 192)
        self.SMG = At("SMG", [128, 16 * 128], BF16, 192)
        self.SMU = At("SMU", [128, 16 * 128], BF16, 196)
        self.SMD = At("SMD", [128, 2048], BF16, 200)
        self.CWR = [At("CWR0", [1, 512], BF16, 86.0), At("CWR1", [1, 512], BF16, 88.0)]
        self.s1_sb = At("s1_sb", [128, 1], F32, 204.25)
        self.s2_sb = At("s2_sb", [128, 1], F32, 204.3125)
        self.m_sb = At("m_sb", [128, 1], F32, 204.375)
        self.v_sb = At("v_sb", [128, 1], F32, 204.4375)
        self.t_sb = At("t_sb", [128, 1], F32, 204.5)
        self.rstd_sb = At("rstd_sb", [128, 1], F32, 204.5625)

    def declare_psum(self):
        nc = self.nc
        self.PB1 = nc.alloc_psum_tensor("PB1", [128, 1024], F32).ap()
        self.PB2 = nc.alloc_psum_tensor("PB2", [128, 1024], F32).ap()
        self.PSM = nc.alloc_psum_tensor("PSM", [128, 1024], F32).ap()
        self.PS1 = self.PSM[:, 0:512]
        self.PS2 = self.PSM[:, 512:1024]
        self.PS3 = nc.alloc_psum_tensor("PS3", [128, 512], F32).ap()
        self.PT = nc.alloc_psum_tensor("PT", [128, 1024], BF16).ap()
        assert nc.psum_banks_remaining == 0, nc.psum_banks_remaining

    def declare_sems(self):
        nc = self.nc
        mk = lambda n: Sem(nc, n)
        self.s_pe = mk("s_pe")
        self.s_act = mk("s_act")
        self.s_dve = mk("s_dve")
        self.s_misc = mk("s_misc")
        self.s_xb = mk("s_xb")
        self.s_qx = mk("s_qx")
        self.s_w1 = mk("s_w1")
        self.s_w2 = mk("s_w2")
        self.s_xf = [mk("s_xf0"), mk("s_xf1")]
        self.s_kst = [mk("s_kst%d" % i) for i in range(4)]
        self.s_kh = [mk("s_kh0"), mk("s_kh1")]
        self.s_vh = [mk("s_vh0"), mk("s_vh1")]
        self.s_vst = [mk("s_vst0"), mk("s_vst1")]
        self.s_mask = mk("s_mask")
        self.s_qxt = [mk("s_qxt0"), mk("s_qxt1")]
        self.s_x2d = mk("s_x2d")
        self.s_cwd = mk("s_cwd")
        self.s_wgu = [mk("s_wgu%d" % i) for i in range(4)]
        self.s_wd = [mk("s_wd0"), mk("s_wd1")]
        self.s_wsm = mk("s_wsm")
        self.s_cwr = [mk("s_cwr0"), mk("s_cwr1")]
        self.s_ain = [mk("s_ain0"), mk("s_ain1")]
        self.s_aout = [mk("s_aout0"), mk("s_aout1")]
        self.s_out = mk("s_out")

    def ph_consts(self):
        s = self.s_misc
        self.dma("sync", self.ident_sb[:, :], self.ident_bf[:, :], s)
        self.dma("sync", self.ident32_sb[:, :], self.ident32[:, :], s)
        self.dma("sync", self.ones_sb[:, :], self.ones_col[:, :], s)
        self.dma("sync", self.onesr32_sb[:, :], self.onesr32[:, :], s)
        self.dma("sync", self.onesrbf_sb[:, :], self.onesrbf[:, :], s)
        self.dma("sync", self.bq_sb[:, :], self.bq[:, :], s)
        self.dma("sync", self.bk_sb[:, :], self.bk[:, :], s)
        self.dma("sync", self.bv_sb[:, :], self.bv[:, :], s)
        self.consts_done = s.n
        self.dma("gpsimd",
                 self.gw_sb[:, :].rearrange("p (c e) -> p c e", c=NH),
                 self.gwt.rearrange("(c p) e -> p c e", p=128), self.s_w1)
        self.gw_done = self.s_w1.n

    def ph_casts(self):
        self.dma("gpsimd", self.xb_bf[0:512, :], self.xb[0:512, :], self.s_xb)
        self.dma("gpsimd",
                 self.W1[:, :].rearrange("p (c f) -> p c f", c=16),
                 self.Wk.rearrange("(c p) f -> p c f", p=128), self.s_w1)
        self.dma("gpsimd",
                 self.Wv_sb[:, :].rearrange("p (c f) -> p c f", c=16),
                 self.Wv.rearrange("(c p) f -> p c f", p=128), self.s_w2)
        for kt in range(1, 8):
            self.dma("gpsimd", self.xb_bf[kt * 512:(kt + 1) * 512, :],
                     self.xb[kt * 512:(kt + 1) * 512, :], self.s_xb)
        self.dma("gpsimd", self.qx_bf[:, :], self.qx[:, :], self.s_qx)

    def emit_xfm_tile(self, src_dram, kt, slot, pe_free):
        if not hasattr(self, "strip_fill"):
            self.strip_fill = {}
            self.strip_g = 0
            self.pt_evac_dve = 0
        first_evac = True
        for s in range(4):
            g = self.strip_g
            self.strip_g += 1
            sb = self.XSTR[g % 2]
            self.wait("sync", self.s_pe, self.strip_fill.get(g - 2, 0))
            self.dma("sync", sb[:, :],
                     src_dram[kt * 512 + s * 128: kt * 512 + (s + 1) * 128, :],
                     self.s_xf[slot])
            ld = self.s_xf[slot].n
            for half in range(2):
                self.wait("tensor", self.s_xf[slot], ld)
                self.wait("tensor", self.s_dve, self.pt_evac_dve)
                pe_done = 0
                for i in range(8):
                    hc = half * 8 + i
                    pe_done = self.pe(lambda e, sb=sb, hc=hc, i=i:
                                      e.transpose(self.PT[:, i * 128:(i + 1) * 128],
                                                  sb[:, hc * 128:(hc + 1) * 128],
                                                  self.ident_sb[:, :]),
                                      inc=(i == 7))
                self.strip_fill[g] = pe_done
                self.wait("vector", self.s_pe, pe_done)
                if first_evac:
                    self.wait("vector", self.s_pe, pe_free)
                    first_evac = False
                self.pt_evac_dve = self.dve(
                    lambda e, slot=slot, half=half, s=s:
                    e.tensor_copy(out=self.XF[slot][:, :]
                                  .rearrange("p (c t) -> p c t", c=16)
                                  [:, half * 8:(half + 1) * 8, s * 128:(s + 1) * 128],
                                  in_=self.PT[:, :].rearrange("p (c t) -> p c t", c=8)))
        return self.s_dve.n

    def ph_A(self):
        kslots = [self.PS1, self.PS2, self.PS3]
        kslot_free = [0, 0, 0]
        vps = [self.PB1[:, 0:512], self.PB1[:, 512:1024],
               self.PB2[:, 0:512], self.PB2[:, 512:1024]]
        kst_fill = [0] * 4
        vst_fill = [0, 0]
        ki = 0
        xf_ready = [0, 0]
        self.pe_a_done = {}
        vevac_dve = 0
        self.dma("sync", self.bvrow32_sb[0:1, :], self.bv_row[0:1, :], self.s_misc)
        bv_ld = self.s_misc.n
        self.wait("sync", self.s_xb, 16)
        xf_ready[0] = self.emit_xfm_tile(self.xb_bf, 0, 0, 0)
        self.wait("tensor", self.s_misc, self.consts_done)
        self.wait("tensor", self.s_misc, bv_ld)
        self.wait("tensor", self.s_w1, self.s_w1.n)
        self.wait("tensor", self.s_w2, self.s_w2.n)
        for nb in range(4):
            bc = self.pe(lambda e, nb=nb:
                         e.matmul(vps[nb], self.onesr32_sb[0:1, :],
                                  self.bvrow32_sb[0:1, nb * 512:(nb + 1) * 512],
                                  start=True, stop=True, skip_group_check=True), inc=True)
            self.wait("scalar", self.s_pe, bc)
            self.act(lambda e, nb=nb: e.activation(out=self.bvtile_sb[:, nb * 512:(nb + 1) * 512],
                                                   in_=vps[nb], func=AF.Copy))
        bvt_act = self.s_act.n
        for kt in range(8):
            sx = kt % 2
            self.wait("tensor", self.s_dve, xf_ready[sx])
            for grp in range(4):
                for fi in range(4):
                    fc = grp * 4 + fi
                    ps = kslots[ki % 3]
                    self.wait("tensor", self.s_act, kslot_free[ki % 3])
                    pe_done = 0
                    for hc in range(16):
                        pe_done = self.pe(
                            lambda e, ps=ps, hc=hc, fc=fc, sx=sx:
                            e.matmul(ps, self.W1[:, hc * 2048 + fc * 128: hc * 2048 + (fc + 1) * 128],
                                     self.XF[sx][:, hc * 512:(hc + 1) * 512],
                                     start=(hc == 0), stop=(hc == 15), skip_group_check=True),
                            inc=(hc == 15))
                    ks = ki % 4
                    self.wait("scalar", self.s_pe, pe_done)
                    self.wait("scalar", self.s_kst[ks], 16 * kst_fill[ks])
                    a_tgt = self.act(lambda e, ps=ps, fc=fc, ks=ks:
                                     e.activation(out=self.KST[ks][:, :], in_=ps, func=AF.Identity,
                                                  bias=self.bk_sb[:, fc:fc + 1]))
                    kslot_free[ki % 3] = a_tgt
                    self.wait("sync", self.s_act, a_tgt)
                    self.dma("sync", self.K_dram[fc][:, kt * 512:(kt + 1) * 512],
                             self.KST[ks][:, :], self.s_kst[ks])
                    kst_fill[ks] += 1
                    ki += 1
                sub = grp
                self.wait("tensor", self.s_dve, vevac_dve)
                self.wait("tensor", self.s_act, bvt_act)
                pe_done = 0
                for hc in range(16):
                    for nb in range(4):
                        pe_done = self.pe(
                            lambda e, hc=hc, nb=nb, sub=sub, sx=sx:
                            e.matmul(vps[nb],
                                     self.XF[sx][:, hc * 512 + sub * 128: hc * 512 + (sub + 1) * 128],
                                     self.Wv_sb[:, hc * 2048 + nb * 512: hc * 2048 + (nb + 1) * 512],
                                     start=(hc == 0), stop=(hc == 15), skip_group_check=True),
                            inc=(hc == 15 and nb == 3))
                vs = (kt * 4 + sub) % 2
                self.wait("vector", self.s_pe, pe_done)
                self.wait("vector", self.s_act, bvt_act)
                self.wait("vector", self.s_vst[vs], 16 * vst_fill[vs])
                for half in range(2):
                    ps = self.PB1 if half == 0 else self.PB2
                    vevac_dve = self.dve(lambda e, ps=ps, vs=vs, half=half:
                                         e.tensor_tensor(out=self.VST[vs][:, half * 1024:(half + 1) * 1024],
                                                         in0=ps[:, :],
                                                         in1=self.bvtile_sb[:, half * 1024:(half + 1) * 1024],
                                                         op=ALU.add))
                self.wait("sync", self.s_dve, vevac_dve)
                self.dma("sync", self.V_dram[kt * 512 + sub * 128: kt * 512 + (sub + 1) * 128, :],
                         self.VST[vs][:, :], self.s_vst[vs])
                vst_fill[vs] += 1
            self.pe_a_done[kt] = self.s_pe.n
            if kt + 1 < 8:
                ns = (kt + 1) % 2
                self.wait("sync", self.s_xb, 16 * (kt + 2))
                xf_ready[ns] = self.emit_xfm_tile(self.xb_bf, kt + 1, ns,
                                                  self.pe_a_done.get(kt - 1, 0))
        self.a_vst_total = [self.s_vst[0].n, self.s_vst[1].n]
        self.a_kslot_free = list(kslot_free)

    def ph_A2(self):
        self.wait("gpsimd", self.s_pe, self.s_pe.n)
        self.dma("gpsimd", self.W1[:, :].rearrange("p (c f) -> p c f", c=16),
                 self.Wq.rearrange("(c p) f -> p c f", p=128), self.s_w1)
        wq_done = self.s_w1.n
        self.wait("sync", self.s_qx, 16)
        xf_ready = [0, 0]
        for qt in range(2):
            xf_ready[qt] = self.emit_xfm_tile(self.qx_bf, qt, qt, self.pe_a_done[6 + qt])
        self.wait("tensor", self.s_w1, wq_done)
        slots = [self.PS1, self.PS2, self.PS3]
        free_act = list(self.a_kslot_free)
        si = 0
        for qt in range(2):
            self.wait("tensor", self.s_dve, xf_ready[qt])
            for fc in range(16):
                ps = slots[si % 3]
                self.wait("tensor", self.s_act, free_act[si % 3])
                pe_done = 0
                for hc in range(16):
                    pe_done = self.pe(
                        lambda e, ps=ps, hc=hc, fc=fc, qt=qt:
                        e.matmul(ps, self.W1[:, hc * 2048 + fc * 128: hc * 2048 + (fc + 1) * 128],
                                 self.XF[qt][:, hc * 512:(hc + 1) * 512],
                                 start=(hc == 0), stop=(hc == 15), skip_group_check=True),
                        inc=(hc == 15))
                self.wait("scalar", self.s_pe, pe_done)
                a_tgt = self.act(lambda e, ps=ps, fc=fc, qt=qt:
                                 e.activation(out=self.Q_sb[:, fc * 1024 + qt * 512:
                                                            fc * 1024 + qt * 512 + 512],
                                              in_=ps, func=AF.Identity,
                                              bias=self.bq_sb[:, fc:fc + 1]))
                free_act[si % 3] = a_tgt
                si += 1
        self.q_done_act = self.s_act.n

    def ph_B(self):
        PTf = self.PT.bitcast(F32)
        self.wait("sync", self.s_pe, self.s_pe.n)
        self.dma("sync", self.masks_sb[:, :].rearrange("p (k q) -> p k q", k=32),
                 self.masks.rearrange("k p q -> p k q"), self.s_mask)
        for st in self.s_kst:
            self.wait("sync", st, st.n)
        kh_fill = [0, 0]
        vt_fill = [0, 0]
        for sv in self.s_vst:
            self.wait("sync", sv, sv.n)
        self.dma("sync", self.KH[0][:, :], self.K_dram[0][:, :], self.s_kh[0])
        kh_fill[0] = 1
        self.dma("sync", self.VT[0][:, :].rearrange("p (c d) -> p c d", c=32),
                 self.V_dram[:, 0:128].rearrange("(c p) d -> p c d", p=128), self.s_vh[0])
        vt_fill[0] = 1
        self.wait("tensor", self.s_mask, 16)
        self.wait("tensor", self.s_act, self.q_done_act)
        self.wait("tensor", self.s_dve, self.s_dve.n)
        sc_slots = [self.PB2, self.PSM]
        sslot_free = [0, 0]
        exp_tgt = {}
        mask_tgt = {}
        ctx_free_dve = 0
        den_free_act = 0
        self.head_pe_done = {}
        for h in range(NH):
            sl = h % 2
            if h + 1 < NH:
                ns = (h + 1) % 2
                if h >= 1:
                    self.wait("sync", self.s_pe, self.head_pe_done[h - 1])
                self.dma("sync", self.KH[ns][:, :], self.K_dram[h + 1][:, :], self.s_kh[ns])
                kh_fill[ns] += 1
                self.dma("sync", self.VT[ns][:, :].rearrange("p (c d) -> p c d", c=32),
                         self.V_dram[:, (h + 1) * 128:(h + 2) * 128]
                         .rearrange("(c p) d -> p c d", p=128), self.s_vh[ns])
                vt_fill[ns] += 1
            self.wait("tensor", self.s_kh[sl], 16 * kh_fill[sl])
            self.wait("tensor", self.s_vh[sl], 16 * vt_fill[sl])
            self.wait("tensor", self.s_dve, ctx_free_dve)
            self.wait("tensor", self.s_act, den_free_act)
            den_started = [False, False]
            ctx_started = [False, False]
            for it in range(34):
                if it < 32:
                    kc = it
                    slot = sc_slots[kc % 2]
                    qs = 128 * (kc // 4)
                    self.wait("tensor", self.s_act, sslot_free[kc % 2])
                    pe_cnt = 0
                    for (a0, w) in score_parts(kc):
                        pe_cnt = self.pe(lambda e, slot=slot, sl=sl, kc=kc, a0=a0, w=w, h=h:
                                         e.matmul(slot[:, a0:a0 + w],
                                                  self.KH[sl][:, kc * 128:(kc + 1) * 128],
                                                  self.Q_sb[:, h * 1024 + a0: h * 1024 + a0 + w],
                                                  start=True, stop=True, skip_group_check=True),
                                         inc=True)
                    self.wait("scalar", self.s_pe, pe_cnt)
                    a_tgt = self.act(lambda e, slot=slot, qs=qs, es=kc % 3:
                                     e.activation(out=self.EXP[:, es * 1024 + qs: (es + 1) * 1024],
                                                  in_=slot[:, qs:1024], func=AF.Exp,
                                                  scale=INV_SQRT_DH))
                    sslot_free[kc % 2] = a_tgt
                    exp_tgt[kc] = a_tgt
                    self.wait("vector", self.s_act, a_tgt)
                    mask_tgt[kc] = self.dve(
                        lambda e, es=kc % 3, qs=qs, kc=kc:
                        e.tensor_tensor(out=self.EXP[:, es * 1024 + qs: es * 1024 + qs + 128],
                                        in0=self.EXP[:, es * 1024 + qs: es * 1024 + qs + 128],
                                        in1=self.masks_sb[:, kc * 128:(kc + 1) * 128],
                                        op=ALU.mult))
                j = it - 2
                if 0 <= j < 32:
                    self.wait("tensor", self.s_dve, mask_tgt[j])
                    for (a0, w) in score_parts(j):
                        bank = 0 if a0 < 512 else 1
                        cstart = not ctx_started[bank]
                        ctx_started[bank] = True
                        dstart = not den_started[bank]
                        den_started[bank] = True
                        clast = (j == 15 and bank == 0) or (j == 31 and bank == 1)
                        self.pe(lambda e, j=j, a0=a0, w=w, es=j % 3, st=cstart, sp=clast, sl=sl:
                                e.matmul(self.PB1[:, a0:a0 + w],
                                         self.VT[sl][:, j * 128:(j + 1) * 128],
                                         self.EXP[:, es * 1024 + a0: es * 1024 + a0 + w],
                                         start=st, stop=sp, skip_group_check=True))
                        dps = self.PS3 if bank == 0 else PTf
                        da0 = a0 - bank * 512
                        self.pe(lambda e, dps=dps, da0=da0, w=w, es=j % 3, a0=a0, st=dstart, sp=clast:
                                e.matmul(dps[0:1, da0:da0 + w], self.ones_sb[:, 0:1],
                                         self.EXP[:, es * 1024 + a0: es * 1024 + a0 + w],
                                         start=st, stop=sp, skip_group_check=True),
                                inc=True)
            head_pe = self.s_pe.n
            self.head_pe_done[h] = head_pe
            self.wait("scalar", self.s_pe, head_pe)
            self.wait("scalar", self.s_dve, getattr(self, "prev_r_tgt", 0))
            for bank, dps in ((0, self.PS3), (1, PTf)):
                den_free_act = self.act(lambda e, dps=dps, bank=bank:
                                        e.activation(out=self.den_sb[0:1, bank * 512:(bank + 1) * 512],
                                                     in_=dps[0:1, 0:512], func=AF.Copy))
            self.wait("vector", self.s_act, den_free_act)
            self.wait("vector", self.s_pe, getattr(self, "prev_bc_pe", 0))
            r_tgt = self.dve(lambda e: e.reciprocal(out=self.recip_sb[0:1, :],
                                                    in_=self.den_sb[0:1, :]))
            self.wait("vector", self.s_dve, r_tgt)
            r_tgt = self.dve(lambda e: e.tensor_copy(out=self.recipbf_sb[0:1, :],
                                                     in_=self.recip_sb[0:1, :]))
            self.prev_r_tgt = r_tgt
            for half in range(2):
                self.wait("tensor", self.s_dve, r_tgt)
                self.wait("tensor", self.s_act, sslot_free[1])
                bc_pe = self.pe(lambda e, half=half:
                                e.matmul(self.PSM[:, half * 512:(half + 1) * 512],
                                         self.onesrbf_sb[0:1, :],
                                         self.recipbf_sb[0:1, half * 512:(half + 1) * 512],
                                         start=True, stop=True, skip_group_check=True),
                                inc=True)
                self.prev_bc_pe = bc_pe
                self.wait("scalar", self.s_pe, bc_pe)
                self.wait("scalar", self.s_dve, ctx_free_dve)
                rb_tgt = self.act(lambda e, half=half:
                                  e.activation(out=self.rb_sb[:, :],
                                               in_=self.PSM[:, half * 512:(half + 1) * 512],
                                               func=AF.Copy))
                sslot_free[1] = rb_tgt
                self.wait("vector", self.s_act, rb_tgt)
                ctx_free_dve = self.dve(lambda e, h=h, half=half:
                                        e.tensor_tensor(
                                            out=self.ctx_sb[:, h * 1024 + half * 512:
                                                            h * 1024 + half * 512 + 512],
                                            in0=self.PB1[:, half * 512:(half + 1) * 512],
                                            in1=self.rb_sb[:, :], op=ALU.mult))
        self.b_done_dve = self.s_dve.n
        self.b_done_act = self.s_act.n

    def emit_ln(self, res_ts, x2_ts, sq_ts):
        v = "vector"
        t1 = self.dve(lambda e, r=res_ts: e.reduce_sum(self.s1_sb[:, :], r, axis=AX.X))
        self.wait("scalar", self.s_dve, t1)
        a1 = self.act(lambda e, r=res_ts, sq=sq_ts:
                      e.activation(out=sq, in_=r, func=AF.Square,
                                   accum_out=self.s2_sb[:, :]))
        self.wait(v, self.s_act, a1)
        t2 = self.dve(lambda e: e.tensor_scalar(out=self.m_sb[:, :], in0=self.s1_sb[:, :],
                                                scalar1=1.0 / 2048.0, scalar2=None, op0=ALU.mult))
        self.wait(v, self.s_dve, t2)
        t3 = self.dve(lambda e: e.tensor_scalar(out=self.t_sb[:, :], in0=self.m_sb[:, :],
                                                scalar1=self.m_sb[:, 0:1], scalar2=None,
                                                op0=ALU.mult))
        t4 = self.dve(lambda e: e.tensor_scalar(out=self.v_sb[:, :], in0=self.s2_sb[:, :],
                                                scalar1=1.0 / 2048.0, scalar2=None, op0=ALU.mult))
        self.wait(v, self.s_dve, t4)
        t5 = self.dve(lambda e: e.tensor_tensor(out=self.v_sb[:, :], in0=self.v_sb[:, :],
                                                in1=self.t_sb[:, :], op=ALU.subtract))
        self.wait(v, self.s_dve, t5)
        t5b = self.dve(lambda e: e.tensor_scalar(out=self.v_sb[:, :], in0=self.v_sb[:, :],
                                                 scalar1=1e-5, scalar2=None, op0=ALU.add))
        self.wait("scalar", self.s_dve, t5b)
        a2 = self.act(lambda e: e.activation(out=self.t_sb[:, :], in_=self.v_sb[:, :],
                                             func=AF.Sqrt))
        self.wait(v, self.s_act, a2)
        t6 = self.dve(lambda e: e.reciprocal(out=self.rstd_sb[:, :], in_=self.t_sb[:, :]))
        self.wait(v, self.s_dve, t6)
        t7 = self.dve(lambda e, r=res_ts, o=x2_ts: e.tensor_scalar(
            out=o, in0=r, scalar1=self.m_sb[:, 0:1], scalar2=self.rstd_sb[:, 0:1],
            op0=ALU.subtract, op1=ALU.mult))
        return t7

    def ph_C(self):
        self.wait("gpsimd", self.s_pe, self.s_pe.n)
        self.dma("gpsimd", self.W1[:, :].rearrange("p (c f) -> p c f", c=16),
                 self.Wo.rearrange("(c p) f -> p c f", p=128), self.s_w1)
        wo_done = self.s_w1.n
        qxt_fill = [0, 0]
        qxt_fill_at = {}
        self.dma("sync", self.TS[0][:, :], self.qx[0:128, :], self.s_qxt[0])
        qxt_fill[0] = 1
        qxt_fill_at[0] = 1
        self.dma("sync", self.TS[1][:, :], self.qx[128:256, :], self.s_qxt[1])
        qxt_fill[1] = 1
        qxt_fill_at[1] = 1
        self.wait("tensor", self.s_w1, wo_done)
        self.wait("tensor", self.s_dve, self.b_done_dve)
        self.wait("tensor", self.s_act, self.b_done_act)
        wo_pe = {}
        res_dve = {}
        ln_x2 = {}
        sq_act = {}
        tb_act = {}
        tb_pe = {}
        lg_copied = {}
        cwtr_dve = {}
        x2d_n = {}
        pt_free = 0

        def res_slot(t):
            return self.TS[2] if t % 2 == 0 else self.TS[4]

        for it in range(9):
            tcr = it - 1
            if 0 <= tcr < 8:
                sl = tcr % 2
                self.wait("vector", self.s_pe, wo_pe[tcr])
                self.wait("vector", self.s_qxt[sl], 16 * qxt_fill_at[tcr])
                if tcr >= 2:
                    self.wait("vector", self.s_dve, ln_x2.get(tcr - 2, 0))
                for half in range(2):
                    ps = self.PB1 if half == 0 else self.PB2
                    self.dve(lambda e, ps=ps, sl=sl, half=half, tcr=tcr:
                             e.tensor_tensor(out=res_slot(tcr)[:, half * 1024:(half + 1) * 1024],
                                             in0=ps[:, :],
                                             in1=self.TS[sl][:, half * 1024:(half + 1) * 1024],
                                             op=ALU.add))
                res_dve[tcr] = self.s_dve.n
            if it < 8:
                tc = it
                if tc >= 1:
                    self.wait("tensor", self.s_dve, res_dve[tc - 1])
                pe_c = 0
                for hc in range(16):
                    for nb in range(4):
                        ps = self.PB1 if nb < 2 else self.PB2
                        off = (nb % 2) * 512
                        pe_c = self.pe(
                            lambda e, hc=hc, tc=tc, ps=ps, off=off, nb=nb:
                            e.matmul(ps[:, off:off + 512],
                                     self.ctx_sb[:, hc * 1024 + tc * 128: hc * 1024 + (tc + 1) * 128],
                                     self.W1[:, hc * 2048 + nb * 512: hc * 2048 + (nb + 1) * 512],
                                     start=(hc == 0), stop=(hc == 15), skip_group_check=True),
                            inc=(hc == 15 and nb == 3))
                wo_pe[tc] = pe_c
                self.last_wo_pe = pe_c
            if 0 <= tcr < 8:
                sl = tcr % 2
                self.wait("vector", self.s_dve, res_dve[tcr])
                if tcr >= 1:
                    self.wait("vector", self.s_x2d, x2d_n[tcr - 1])
                    self.wait("vector", self.s_act, tb_act.get(tcr - 1, 0))
                x2_t = self.emit_ln(res_slot(tcr)[:, :], self.TS[3][:, :], self.TS[sl][:, :])
                ln_x2[tcr] = x2_t
                sq_act[tcr] = self.s_act.n
                if tcr + 2 < 8:
                    self.wait("sync", self.s_act, sq_act[tcr])
                    self.dma("sync", self.TS[sl][:, :],
                             self.qx[(tcr + 2) * 128:(tcr + 3) * 128, :], self.s_qxt[sl])
                    qxt_fill[sl] += 1
                    qxt_fill_at[tcr + 2] = qxt_fill[sl]
                self.wait("sync", self.s_dve, x2_t)
                self.dma("sync", self.x2_dram[tcr * 128:(tcr + 1) * 128, :], self.TS[3][:, :],
                         self.s_x2d)
                x2d_n[tcr] = self.s_x2d.n
                self.wait("scalar", self.s_dve, x2_t)
                if tcr >= 1:
                    self.wait("scalar", self.s_pe, tb_pe.get(tcr - 1, 0))
                tb_t = self.act(lambda e: e.activation(out=self.TB[:, :], in_=self.TS[3][:, :],
                                                       func=AF.Copy))
                tb_act[tcr] = tb_t
                self.wait("tensor", self.s_act, tb_t)
                for grp in range(2):
                    self.wait("tensor", self.s_dve, pt_free)
                    pe_done = 0
                    for i in range(8):
                        hc = grp * 8 + i
                        pe_done = self.pe(lambda e, hc=hc, i=i:
                                          e.transpose(self.PT[:, i * 128:(i + 1) * 128],
                                                      self.TB[:, hc * 128:(hc + 1) * 128],
                                                      self.ident_sb[:, :]),
                                          inc=(i == 7))
                    self.wait("vector", self.s_pe, pe_done)
                    pt_free = self.dve(lambda e, g=grp, tc=tcr:
                                       e.tensor_copy(
                                           out=self.x2f_sb[:, :]
                                           .rearrange("p (c t) -> p c t", c=16)
                                           [:, g * 8:(g + 1) * 8, tc * 128:(tc + 1) * 128],
                                           in_=self.PT[:, :].rearrange("p (c t) -> p c t", c=8)))
                tb_pe[tcr] = self.s_pe.n
                x2f_dve = self.s_dve.n
                self.wait("tensor", self.s_dve, x2f_dve)
                self.wait("tensor", self.s_w1, self.gw_done)
                if tcr >= 1:
                    self.wait("tensor", self.s_dve, lg_copied.get(tcr - 1, 0))
                lg_pe = 0
                for hc in range(16):
                    lg_pe = self.pe(lambda e, hc=hc, tc=tcr:
                                    e.matmul(self.PS2[:, 0:NE],
                                             self.x2f_sb[:, hc * 1024 + tc * 128:
                                                         hc * 1024 + (tc + 1) * 128],
                                             self.gw_sb[:, hc * NE:(hc + 1) * NE],
                                             start=(hc == 0), stop=(hc == 15),
                                             skip_group_check=True),
                                    inc=(hc == 15))
                self.wait("vector", self.s_pe, lg_pe)
                v = "vector"

                def step(fn):
                    t = self.dve(fn)
                    self.wait(v, self.s_dve, t)
                    return t

                lgc = step(lambda e: e.tensor_copy(out=self.lg_sb[:, :], in_=self.PS2[:, 0:NE]))
                lg_copied[tcr] = lgc
                step(lambda e: e.tensor_copy(out=self.work_sb[:, :], in_=self.lg_sb[:, :]))
                for r in range(4):
                    step(lambda e, r=r: e.reduce_max(self.mpos_sb[:, r:r + 1],
                                                     self.work_sb[:, :], axis=AX.X))
                    if r < 3:
                        step(lambda e, r=r: e.tensor_scalar(
                            out=self.eq_sb[:, :], in0=self.work_sb[:, :],
                            scalar1=self.mpos_sb[:, r:r + 1], scalar2=None, op0=ALU.is_equal))
                        step(lambda e: e.scalar_tensor_tensor(
                            out=self.work_sb[:, :], in0=self.eq_sb[:, :], scalar=-1e30,
                            in1=self.work_sb[:, :], op0=ALU.mult, op1=ALU.add))
                step(lambda e: e.tensor_scalar(
                    out=self.msk_sb[:, :], in0=self.lg_sb[:, :],
                    scalar1=self.mpos_sb[:, 3:4], scalar2=None, op0=ALU.is_ge))
                ee_t = step(lambda e: e.tensor_scalar(
                    out=self.ex_sb[:, :], in0=self.lg_sb[:, :],
                    scalar1=self.mpos_sb[:, 0:1], scalar2=None, op0=ALU.subtract))
                self.wait("scalar", self.s_dve, ee_t)
                aex = self.act(lambda e: e.activation(out=self.ex_sb[:, :],
                                                      in_=self.ex_sb[:, :], func=AF.Exp))
                self.wait(v, self.s_act, aex)
                step(lambda e: e.tensor_tensor(out=self.wm_sb[:, :], in0=self.ex_sb[:, :],
                                               in1=self.msk_sb[:, :], op=ALU.mult))
                step(lambda e: e.reduce_sum(self.rsc_sb[:, 0:1], self.wm_sb[:, :], axis=AX.X))
                step(lambda e: e.reciprocal(out=self.rsc_sb[:, 1:2], in_=self.rsc_sb[:, 0:1]))
                step(lambda e: e.tensor_scalar(
                    out=self.cwt_sb[:, :], in0=self.wm_sb[:, :],
                    scalar1=self.rsc_sb[:, 1:2], scalar2=None, op0=ALU.mult))
                step(lambda e: e.tensor_tensor(
                    out=self.cwst_sb[:, :], in0=self.cwt_sb[:, 0:NE:2],
                    in1=self.cwt_sb[:, 1:NE:2], op=ALU.add))
                cw_t = step(lambda e: e.tensor_scalar(
                    out=self.cwst_sb[:, :], in0=self.cwst_sb[:, :], scalar1=0.05,
                    scalar2=None, op0=ALU.mult))
                self.wait("tensor", self.s_dve, cw_t)
                if tcr >= 1:
                    self.wait("tensor", self.s_dve, cwtr_dve.get(tcr - 1, 0))
                self.pe(lambda e: e.transpose(self.PS3[0:32, 0:128], self.cwt_sb[:, :],
                                              self.ident32_sb[:, :]))
                cwtr_pe = self.pe(lambda e: e.transpose(self.PS3[0:16, 128:256],
                                                        self.cwst_sb[:, :],
                                                        self.ident32_sb[:, :]), inc=True)
                self.wait(v, self.s_pe, cwtr_pe)
                self.dve(lambda e: e.tensor_copy(out=self.cwtr_sb[:, :], in_=self.PS3[0:32, 0:128]))
                ctr2 = self.dve(lambda e: e.tensor_copy(out=self.cwstr_sb[:, :],
                                                        in_=self.PS3[0:16, 128:256]))
                cwtr_dve[tcr] = ctr2
                self.wait("sync", self.s_dve, ctr2)
                self.dma("sync", self.cw_dram[:, tcr * 128:(tcr + 1) * 128], self.cwtr_sb[:, :],
                         self.s_cwd)
                self.dma("sync", self.cws_dram[:, tcr * 128:(tcr + 1) * 128], self.cwstr_sb[:, :],
                         self.s_cwd)
        self.c_done_pe = self.s_pe.n
        self.c_done_dve = self.s_dve.n
        self.c_done_act = self.s_act.n
        self.c_cwd_total = self.s_cwd.n
        self.x2d_total = self.s_x2d.n

    def ph_D(self):
        wgu_fill = [0] * 4
        wgu_free = [0] * 4
        wd_fill = [0, 0]
        wd_free = [0, 0]
        sm_free = 0
        cwr_fill = [0, 0]
        cwr_free = [0, 0]
        ain_fill = [0, 0]
        aout_fill = [0, 0]
        ein_free = [0, 0]
        eout_free = [0, 0]
        prev_eg_aout = {}
        g_free_act = 0
        u_free_dve = 0
        cw_ps3_free = 0
        cwb_free_dve = {}
        aeg_wait = 0
        down_free = [0, 0]

        self.wait("gpsimd", self.s_pe, self.c_done_pe)
        self.wait("gpsimd", self.s_dve, self.c_done_dve)
        self.wait("gpsimd", self.s_act, self.c_done_act)
        self.wait("gpsimd", self.s_x2d, self.x2d_total)
        self.wait("sync", self.s_cwd, self.c_cwd_total)
        self.wait("sync", self.s_x2d, self.x2d_total)
        self.wait("tensor", self.s_dve, self.c_done_dve)

        def load_wgu_half(W, e_glob, hf, slot):
            self.wait("gpsimd", self.s_pe, wgu_free[slot])
            self.dma("gpsimd", self.WGU[slot][:, :].rearrange("p (c i) -> p c i", c=16),
                     W[e_glob].rearrange("(c p) i -> p c i", p=128)
                     [:, :, hf * 384:(hf + 1) * 384], self.s_wgu[slot])
            wgu_fill[slot] += 1

        def load_wd(e_glob, slot):
            self.wait("gpsimd", self.s_pe, wd_free[slot])
            self.dma("gpsimd", self.WD[slot][:, :].rearrange("p (c f) -> p c f", c=6),
                     self.Wd[e_glob].rearrange("(c p) f -> p c f", p=128), self.s_wd[slot])
            wd_fill[slot] += 1

        def load_small(sc):
            self.wait("gpsimd", self.s_pe, sm_free)
            self.dma("gpsimd", self.SMG[:, :].rearrange("p (c i) -> p c i", c=16),
                     self.cWg[sc].rearrange("(c p) i -> p c i", p=128), self.s_wsm)
            self.dma("gpsimd", self.SMU[:, :].rearrange("p (c i) -> p c i", c=16),
                     self.cWu[sc].rearrange("(c p) i -> p c i", p=128), self.s_wsm)
            self.dma("gpsimd", self.SMD[:, :], self.cWd[sc][0:128, :], self.s_wsm)

        def load_cwr(src, row, tile, slot):
            self.wait("sync", self.s_pe, cwr_free[slot])
            self.dma("sync", self.CWR[slot][0:1, :],
                     src[row:row + 1, tile * 512:(tile + 1) * 512], self.s_cwr[slot])
            cwr_fill[slot] += 1

        cw_seq = []
        for eg in range(self.neg):
            for ei in range(2):
                for tile in range(2):
                    cw_seq.append((self.cw_dram, 2 * eg + ei, tile))
            for tile in range(2):
                cw_seq.append((self.cws_dram, eg, tile))
        cw_loaded = 0

        def ensure_cwr(idx):
            nonlocal cw_loaded
            while cw_loaded <= idx + 1 and cw_loaded < len(cw_seq):
                src, row, tile = cw_seq[cw_loaded]
                load_cwr(src, row, tile, cw_loaded % 2)
                cw_loaded += 1

        def bcast_cw(idx, half):
            nonlocal cw_ps3_free
            slot = idx % 2
            ensure_cwr(idx)
            self.wait("tensor", self.s_cwr[slot], 16 * (idx // 2 + 1))
            self.wait("tensor", self.s_act, cw_ps3_free)
            self.wait("tensor", self.s_dve, cwb_free_dve.get(half, 0))
            bc_pe = self.pe(lambda e, slot=slot:
                            e.matmul(self.PS3[:, :], self.onesrbf_sb[0:1, :],
                                     self.CWR[slot][0:1, :], start=True, stop=True,
                                     skip_group_check=True), inc=True)
            cwr_free[slot] = bc_pe
            self.wait("scalar", self.s_pe, bc_pe)
            cw_ps3_free = self.act(lambda e, half=half:
                                   e.activation(out=self.cwb_sb[:, half * 512:(half + 1) * 512],
                                                in_=self.PS3[:, :], func=AF.Copy))
            return cw_ps3_free

        load_wgu_half(self.Wg, 0, 0, 0)
        load_wgu_half(self.Wu, 0, 0, 1)
        load_wgu_half(self.Wg, 0, 1, 2)
        load_wgu_half(self.Wu, 0, 1, 3)
        load_wd(0, 0)
        load_wd(1, 1)
        load_small(0)
        first_unit = True
        d_last_pe = 0
        for eg in range(self.neg):
            for ei in range(2):
                e_glob = 2 * eg + ei
                seq_base = 6 * eg + 2 * ei
                cw_acts = [bcast_cw(seq_base + 0, 0), bcast_cw(seq_base + 1, 1)]
                nxt = e_glob + 1
                if nxt < NE:
                    pass
                for fc in range(6):
                    hf = fc // 3
                    gslot = 2 * hf
                    uslot = 2 * hf + 1
                    self.wait("tensor", self.s_wgu[gslot], 16 * wgu_fill[gslot])
                    self.wait("tensor", self.s_wgu[uslot], 16 * wgu_fill[uslot])
                    for tile in range(2):
                        self.wait("tensor", self.s_act, g_free_act)
                        g_pe = 0
                        for hc in range(16):
                            g_pe = self.pe(
                                lambda e, hc=hc, fc=fc, hf=hf, gslot=gslot, tile=tile:
                                e.matmul(self.PS1[:, :],
                                         self.WGU[gslot][:, hc * 384 + (fc % 3) * 128:
                                                         hc * 384 + (fc % 3 + 1) * 128],
                                         self.x2f_sb[:, hc * 1024 + tile * 512:
                                                     hc * 1024 + (tile + 1) * 512],
                                         start=(hc == 0), stop=(hc == 15),
                                         skip_group_check=True),
                                inc=(hc == 15))
                        self.wait("scalar", self.s_pe, g_pe)
                        g_free_act = self.act(lambda e: e.activation(
                            out=self.SG[:, :], in_=self.PS1[:, :], func=AF.Silu))
                        self.wait("tensor", self.s_dve, u_free_dve)
                        u_pe = 0
                        for hc in range(16):
                            u_pe = self.pe(
                                lambda e, hc=hc, fc=fc, hf=hf, uslot=uslot, tile=tile:
                                e.matmul(self.PS2[:, :],
                                         self.WGU[uslot][:, hc * 384 + (fc % 3) * 128:
                                                         hc * 384 + (fc % 3 + 1) * 128],
                                         self.x2f_sb[:, hc * 1024 + tile * 512:
                                                     hc * 1024 + (tile + 1) * 512],
                                         start=(hc == 0), stop=(hc == 15),
                                         skip_group_check=True),
                                inc=(hc == 15))
                        self.wait("vector", self.s_pe, u_pe)
                        self.wait("vector", self.s_act, g_free_act)
                        self.wait("vector", self.s_act, cw_acts[tile])
                        t1 = self.dve(lambda e: e.tensor_tensor(
                            out=self.T1[:, :], in0=self.SG[:, :], in1=self.PS2[:, :],
                            op=ALU.mult))
                        u_free_dve = t1
                        self.wait("vector", self.s_dve, t1)
                        t2 = self.dve(lambda e, ei=ei, fc=fc, tile=tile: e.tensor_tensor(
                            out=self.AEG[:, (ei * 6 + fc) * 1024 + tile * 512:
                                         (ei * 6 + fc) * 1024 + (tile + 1) * 512],
                            in0=self.T1[:, :], in1=self.cwb_sb[:, tile * 512:(tile + 1) * 512],
                            op=ALU.mult))
                        cwb_free_dve[tile] = t2
                    if fc % 3 == 2:
                        wgu_free[gslot] = self.s_pe.n
                        wgu_free[uslot] = self.s_pe.n
                        if e_glob + 1 < 2 * self.neg:
                            load_wgu_half(self.Wg, e_glob + 1, hf, gslot)
                            load_wgu_half(self.Wu, e_glob + 1, hf, uslot)
            sc = eg
            seq_sc = 6 * eg + 4
            cw_acts = [bcast_cw(seq_sc + 0, 0), bcast_cw(seq_sc + 1, 1)]
            self.wait("tensor", self.s_wsm, 16 * 3 * (eg + 1))
            for tile in range(2):
                self.wait("tensor", self.s_act, g_free_act)
                g_pe = 0
                for hc in range(16):
                    g_pe = self.pe(lambda e, hc=hc, tile=tile:
                                   e.matmul(self.PS1[:, :],
                                            self.SMG[:, hc * 128:(hc + 1) * 128],
                                            self.x2f_sb[:, hc * 1024 + tile * 512:
                                                        hc * 1024 + (tile + 1) * 512],
                                            start=(hc == 0), stop=(hc == 15),
                                            skip_group_check=True),
                                   inc=(hc == 15))
                self.wait("scalar", self.s_pe, g_pe)
                g_free_act = self.act(lambda e: e.activation(
                    out=self.SG[:, :], in_=self.PS1[:, :], func=AF.Silu))
                self.wait("tensor", self.s_dve, u_free_dve)
                u_pe = 0
                for hc in range(16):
                    u_pe = self.pe(lambda e, hc=hc, tile=tile:
                                   e.matmul(self.PS2[:, :],
                                            self.SMU[:, hc * 128:(hc + 1) * 128],
                                            self.x2f_sb[:, hc * 1024 + tile * 512:
                                                        hc * 1024 + (tile + 1) * 512],
                                            start=(hc == 0), stop=(hc == 15),
                                            skip_group_check=True),
                                   inc=(hc == 15))
                self.wait("vector", self.s_pe, u_pe)
                self.wait("vector", self.s_act, g_free_act)
                self.wait("vector", self.s_act, cw_acts[tile])
                t1 = self.dve(lambda e: e.tensor_tensor(
                    out=self.T1[:, :], in0=self.SG[:, :], in1=self.PS2[:, :], op=ALU.mult))
                u_free_dve = t1
                self.wait("vector", self.s_dve, t1)
                t2 = self.dve(lambda e, tile=tile: e.tensor_tensor(
                    out=self.ASM[:, tile * 512:(tile + 1) * 512],
                    in0=self.T1[:, :], in1=self.cwb_sb[:, tile * 512:(tile + 1) * 512],
                    op=ALU.mult))
                cwb_free_dve[tile] = t2
            aeg_wait = self.s_dve.n
            self.wait("tensor", self.s_wd[0], 16 * wd_fill[0])
            self.wait("tensor", self.s_wd[1], 16 * wd_fill[1])
            self.wait("tensor", self.s_dve, aeg_wait)
            ui = 0
            for tile in range(2):
                for st in range(4):
                    r0 = tile * 512 + st * 128
                    if eg > 0:
                        cnt0, cnt1 = prev_eg_aout[(tile, st)]
                        for fh in range(2):
                            esl = fh
                            self.wait("sync", self.s_dve, ein_free[esl])
                            self.wait("sync", self.s_aout[0], 16 * cnt0)
                            self.wait("sync", self.s_aout[1], 16 * cnt1)
                            self.dma("sync", self.EIN[esl][:, :],
                                     self.acc_dram[r0:r0 + 128, fh * 1024:(fh + 1) * 1024],
                                     self.s_ain[esl])
                            ain_fill[esl] += 1
                    for fh in range(2):
                        pb = self.PB1 if fh == 0 else self.PB2
                        pbi = fh
                        esl = fh
                        self.wait("tensor", self.s_dve, down_free[pbi])
                        dn_pe = 0
                        nmm = 0
                        for ei in range(2):
                            for ic in range(6):
                                for nb in range(2):
                                    nmm += 1
                                    dn_pe = self.pe(
                                        lambda e, ei=ei, ic=ic, nb=nb, fh=fh, pb=pb,
                                        tile=tile, st=st, first=(ei == 0 and ic == 0):
                                        e.matmul(pb[:, nb * 512:(nb + 1) * 512],
                                                 self.AEG[:, (ei * 6 + ic) * 1024 + tile * 512
                                                          + st * 128: (ei * 6 + ic) * 1024
                                                          + tile * 512 + (st + 1) * 128],
                                                 self.WD[ei][:, ic * 2048 + fh * 1024
                                                             + nb * 512: ic * 2048 + fh * 1024
                                                             + (nb + 1) * 512],
                                                 start=first, stop=False,
                                                 skip_group_check=True),
                                        inc=False)
                        for nb in range(2):
                            dn_pe = self.pe(
                                lambda e, nb=nb, fh=fh, pb=pb, tile=tile, st=st:
                                e.matmul(pb[:, nb * 512:(nb + 1) * 512],
                                         self.ASM[:, tile * 512 + st * 128:
                                                  tile * 512 + (st + 1) * 128],
                                         self.SMD[:, fh * 1024 + nb * 512:
                                                  fh * 1024 + (nb + 1) * 512],
                                         start=False, stop=True, skip_group_check=True),
                                inc=(nb == 1))
                        self.wait("vector", self.s_pe, dn_pe)
                        osl = fh
                        self.wait("vector", self.s_aout[osl], 16 * aout_fill[osl])
                        if eg > 0:
                            self.wait("vector", self.s_ain[esl], 16 * ain_fill[esl])
                            dv = self.dve(lambda e, pb=pb, osl=osl, esl=esl:
                                          e.tensor_tensor(out=self.EOUT[osl][:, :],
                                                          in0=pb[:, :],
                                                          in1=self.EIN[esl][:, :],
                                                          op=ALU.add))
                        else:
                            dv = self.dve(lambda e, pb=pb, osl=osl:
                                          e.tensor_copy(out=self.EOUT[osl][:, :], in_=pb[:, :]))
                        down_free[pbi] = dv
                        ein_free[esl] = dv
                        self.wait("sync", self.s_dve, dv)
                        self.dma("sync", self.acc_dram[r0:r0 + 128, fh * 1024:(fh + 1) * 1024],
                                 self.EOUT[osl][:, :], self.s_aout[osl])
                        aout_fill[osl] += 1
                    prev_eg_aout[(tile, st)] = (aout_fill[0], aout_fill[1])
                    ui += 1
            d_last_pe = self.s_pe.n
            wd_free[0] = d_last_pe
            wd_free[1] = d_last_pe
            sm_free = d_last_pe
            if eg + 1 < self.neg:
                load_wd(2 * (eg + 1), 0)
                load_wd(2 * (eg + 1) + 1, 1)
                load_small(eg + 1)
        self.d_done_pe = self.s_pe.n
        self.d_done_dve = self.s_dve.n
        self.d_aout_total = [self.s_aout[0].n, self.s_aout[1].n]

    def ph_E(self):
        self.wait("sync", self.s_aout[0], self.d_aout_total[0])
        self.wait("sync", self.s_aout[1], self.d_aout_total[1])
        self.wait("sync", self.s_pe, self.d_done_pe)
        self.wait("sync", self.s_dve, self.d_done_dve)
        acc_ready = {}
        x2_ready = {}
        acc_free = {0: 0, 1: 0}
        x2_free = {0: 0, 1: 0}
        res_free = 0
        out_n = {}
        ln_done = {}

        def acc_slots(t):
            if t % 2 == 0:
                return [(self.EIN[0][:, :], self.s_ain[0]), (self.EIN[1][:, :], self.s_ain[1])]
            return [(self.TS[4][:, 0:1024], self.s_ain[0]), (self.TS[4][:, 1024:2048], self.s_ain[1])]

        def load_tiles(t):
            par = t % 2
            self.wait("sync", self.s_dve, acc_free[par])
            for fh, (ap, sem) in enumerate(acc_slots(t)):
                self.dma("sync", ap,
                         self.acc_dram[t * 128:(t + 1) * 128, fh * 1024:(fh + 1) * 1024], sem)
            acc_ready[t] = (self.s_ain[0].n, self.s_ain[1].n)
            self.wait("sync", self.s_act, x2_free[par])
            self.dma("sync", self.TS[par][:, :], self.x2_dram[t * 128:(t + 1) * 128, :],
                     self.s_qxt[par])
            x2_ready[t] = self.s_qxt[par].n

        load_tiles(0)
        load_tiles(1)
        for tc in range(8):
            par = tc % 2
            self.wait("vector", self.s_ain[0], acc_ready[tc][0])
            self.wait("vector", self.s_ain[1], acc_ready[tc][1])
            self.wait("vector", self.s_qxt[par], x2_ready[tc])
            self.wait("vector", self.s_dve, res_free)
            slots = acc_slots(tc)
            for half in range(2):
                self.dve(lambda e, par=par, half=half, ap=slots[half][0]:
                         e.tensor_tensor(out=self.TS[2][:, half * 1024:(half + 1) * 1024],
                                         in0=ap,
                                         in1=self.TS[par][:, half * 1024:(half + 1) * 1024],
                                         op=ALU.add))
            r2 = self.s_dve.n
            acc_free[par] = r2
            self.wait("vector", self.s_dve, r2)
            if tc >= 1:
                self.wait("vector", self.s_out, out_n[tc - 1])
            o_t = self.emit_ln(self.TS[2][:, :], self.TS[3][:, :], self.TS[par][:, :])
            ln_done[tc] = o_t
            res_free = o_t
            x2_free[par] = self.s_act.n
            if tc + 2 < 8:
                load_tiles(tc + 2)
            self.wait("sync", self.s_dve, o_t)
            self.dma("sync", self.out[tc * 128:(tc + 1) * 128, :], self.TS[3][:, :], self.s_out)
            out_n[tc] = self.s_out.n
        if self.debug:
            self.wait("sync", self.s_pe, self.s_pe.n)
            self.dma("sync", self.d_K[:, :, :], self.K_dram[:, :, :], self.s_out)
            self.dma("sync", self.d_V[:, :], self.V_dram[:, :], self.s_out)
            self.dma("sync", self.d_x2[:, :], self.x2_dram[:, :], self.s_out)
            self.dma("sync", self.d_cw[:, :], self.cw_dram[:, :], self.s_out)
            self.dma("sync", self.d_cws[:, :], self.cws_dram[:, :], self.s_out)
            self.dma("sync", self.d_acc[:, :], self.acc_dram[:, :], self.s_out)
            self.dma("sync", self.d_aeg[:, :], self.AEG[:, :], self.s_out)
            self.dma("sync", self.d_asm[:, :], self.ASM[:, :], self.s_out)
            self.dma("sync", self.d_cwb[:, :], self.cwb_sb[:, :], self.s_out)
        self.wait("sync", self.s_out, self.s_out.n)

    def ph_debug_early(self):
        if not self.debug:
            return
        self.wait("sync", self.s_act, self.q_done_act)
        self.dma("sync", self.d_Q[:, :], self.Q_sb[:, :], self.s_out)
        self.wait("sync", self.s_dve, self.b_done_dve)
        self.dma("sync", self.d_ctx[:, :], self.ctx_sb[:, :], self.s_out)

    def build(self):
        self.ph_consts()
        self.ph_casts()
        self.ph_A()
        self.ph_A2()
        self.ph_B()
        if self.debug:
            self.ph_debug_early()
        self.ph_C()
        self.ph_D()
        self.ph_E()
        with self.nc.Block() as block:
            for name, reg in (("sync", block.sync), ("gpsimd", block.gpsimd),
                              ("vector", block.vector), ("scalar", block.scalar),
                              ("tensor", block.tensor)):
                fns = self.ops[name]

                def runner(e, fns=fns):
                    for f in fns:
                        f(e)
                reg(runner)
        return self.nc


def build_program(debug=0, neg=NEG):
    return Builder(debug=debug, neg=neg).build()


def stage_inputs(inputs):
    x = np.asarray(inputs["L0_attn_hidden_states"])
    ident_bf = np.eye(128, dtype=bfloat16)
    ident32 = np.eye(128, dtype=np.float32)
    ones_col = np.ones((128, 1), dtype=bfloat16)
    onesr32 = np.ones((1, 128), dtype=np.float32)
    shared = {
        "Wq": np.asarray(inputs["L0_attn_Wq"]),
        "Wk": np.asarray(inputs["L0_attn_Wk"]),
        "Wv": np.asarray(inputs["L0_attn_Wv"]),
        "Wo": np.asarray(inputs["L0_attn_Wo"]),
        "bq": np.ascontiguousarray(np.asarray(inputs["L0_attn_bq"]).reshape(16, 128).T),
        "bk": np.ascontiguousarray(np.asarray(inputs["L0_attn_bk"]).reshape(16, 128).T),
        "bv": np.ascontiguousarray(np.asarray(inputs["L0_attn_bv"]).reshape(16, 128).T),
        "bv_row": np.ascontiguousarray(np.asarray(inputs["L0_attn_bv"]).reshape(1, 2048)),
        "gwt": np.ascontiguousarray(np.asarray(inputs["L0_ffn_gate_w"]).T),
        "Wg": np.asarray(inputs["L0_ffn_Wg"]),
        "Wu": np.asarray(inputs["L0_ffn_Wu"]),
        "Wd": np.asarray(inputs["L0_ffn_Wd"]),
        "cWg": np.asarray(inputs["L0_ffn_cWg"]),
        "cWu": np.asarray(inputs["L0_ffn_cWu"]),
        "cWd": np.asarray(inputs["L0_ffn_cWd"]),
        "ident_bf": ident_bf,
        "ident32": ident32,
        "ones_col": ones_col,
        "onesr32": onesr32,
        "onesrbf": np.ones((1, 128), dtype=bfloat16),
    }
    in_maps = []
    chunk_maps = []
    tri = np.triu(np.ones((128, 128), dtype=np.float32))
    for c in range(NCORE := 8):
        b = c // 4
        g = c % 4
        chunks = chunk_assign(g)
        chunk_maps.append(chunks)
        xb = np.ascontiguousarray(x[b])
        qx = np.ascontiguousarray(
            np.concatenate([x[b, 128 * ci:128 * (ci + 1)] for ci in chunks], axis=0))
        masks = np.zeros((32, 128, 128), dtype=np.float32)
        for kc in range(32):
            j = kc // 4
            p = chunks[j] + 1
            if kc < p - 1:
                masks[kc] = 1.0
            elif kc == p - 1:
                masks[kc] = tri
        m = {"xb": xb, "qx": qx, "masks": masks.astype(bfloat16)}
        m.update(shared)
        in_maps.append(m)
    return in_maps, chunk_maps


def kernel(**inputs):
    in_maps, chunk_maps = stage_inputs(inputs)
    nc = build_program(debug=0)
    res = run_bass_kernel_spmd(nc, in_maps, core_ids=list(range(8)))
    out = np.zeros((2, 4096, 2048), dtype=np.float32)
    for c in range(8):
        b = c // 4
        shard = res.results[c]["out"]
        for j, ci in enumerate(chunk_maps[c]):
            out[b, 128 * ci:128 * (ci + 1), :] = shard[128 * j:128 * (j + 1), :]
    return out
